# version 01
# baseline (speedup 1.0000x reference)
import math
import numpy as np
import ml_dtypes
import concourse.bass as bass
import concourse.mybir as mybir
from concourse.bass_utils import run_bass_kernel_spmd

F32 = mybir.dt.float32
BF16 = mybir.dt.bfloat16
AF = mybir.ActivationFunctionType
ALU = mybir.AluOpType
AX = mybir.AxisListType
BF = ml_dtypes.bfloat16

NCORES = 8
S_PER_CORE = 4
T = 2048
D = 1024
NT = T // 128
ND = D // 128

ENGINES = ("tensor", "vector", "scalar", "gpsimd", "sync")



def _tile128(v):
    v = np.ascontiguousarray(np.asarray(v, np.float32))
    return np.ascontiguousarray(np.broadcast_to(v[None, :], (128, v.shape[0])))


CONST_VALS = [1e-5, 1e-6, 1e-9, 1e-20, 1e-12, 0.0, 1.0]
CONST_IDX = {v: i for i, v in enumerate(CONST_VALS)}


def _stage(ins):
    st = {}
    f32 = np.float32
    st["c_scalars"] = _tile128(np.array(CONST_VALS, f32))
    st["c_ident"] = np.eye(128, dtype=f32)
    m = np.zeros((128, 128), f32)
    m[np.triu_indices(128, k=1)] = -1e30
    st["c_causal0"] = m
    WqkvT = np.ascontiguousarray(np.asarray(ins["L0_attn_W_qkv"]).T)
    pair_cols = []
    for p in range(8):
        h0, h1 = 2 * p, 2 * p + 1
        cols = []
        for part in range(3):
            cols.append(WqkvT[:, 192 * h0 + 64 * part: 192 * h0 + 64 * (part + 1)])
            cols.append(WqkvT[:, 192 * h1 + 64 * part: 192 * h1 + 64 * (part + 1)])
        pair_cols.append(np.concatenate(cols, axis=1))
    st["L0_wqkv_pg"] = np.ascontiguousarray(np.stack(pair_cols))
    st["L0_wo_t"] = np.ascontiguousarray(np.asarray(ins["L0_attn_W_o"]).T)
    st["L0_gamma"] = _tile128(ins["L0_attn_ln_gamma"])
    st["L0_beta"] = _tile128(ins["L0_attn_ln_beta"])
    st["L0f_w_t"] = np.ascontiguousarray(np.asarray(ins["L0_ffn_weights"]).T)
    st["L0f_scale"] = _tile128(ins["L0_ffn_scale"])
    st["L0f_bias"] = _tile128(ins["L0_ffn_bias"])
    a = np.arange(128)
    keep = (a[None, :] < a[:, None]).astype(f32)
    st["c_strict_keep"] = keep
    st["c_strict_fill"] = (1.0 - keep) * np.float32(-1e6)
    Wq = np.asarray(ins["L1_attn_Wq"]); Wk = np.asarray(ins["L1_attn_Wk"]); Wv = np.asarray(ins["L1_attn_Wv"])
    st["L1_wqkv_pg"] = np.ascontiguousarray(np.stack([
        np.concatenate([Wq[:, p * 128:(p + 1) * 128],
                        Wk[:, p * 128:(p + 1) * 128],
                        Wv[:, p * 128:(p + 1) * 128]], axis=1) for p in range(8)]))
    st["L1_wo"] = np.ascontiguousarray(np.asarray(ins["L1_attn_Wo"]))
    st["L1_w1"] = np.ascontiguousarray(np.asarray(ins["L1_attn_W1"]))
    st["L1_w2"] = np.ascontiguousarray(np.asarray(ins["L1_attn_W2"]))
    st["L1_ln1"] = _tile128(ins["L1_attn_ln1_w"])
    st["L1_ln2"] = _tile128(ins["L1_attn_ln2_w"])
    st["L1_scaleh"] = _tile128(ins["L1_attn_scale"])
    NB, MAXD = 32, 128
    d = np.arange(-127, 128 * 15 + 128)
    rp = np.maximum(d, 0)
    safe = np.maximum(rp, 1).astype(np.float32)
    large = 16 + (np.log(safe / np.float32(16.0)) / np.float32(math.log(MAXD / 16)) * np.float32(16.0)).astype(np.int32)
    large = np.minimum(large, NB - 1)
    bucket_d = np.where(rp < 16, rp, large)
    rel = np.asarray(ins["L1_attn_rel_bias"])
    aa = np.arange(128)
    tiles = np.empty((16, 16, 128, 128), f32)
    for t in range(16):
        dmat = 128 * t + aa[:, None] - aa[None, :]
        bt = bucket_d[dmat + 127]
        tiles[:, t] = np.transpose(rel[bt], (2, 0, 1))
    st["L1_btiles"] = np.ascontiguousarray(tiles)
    st["c_iota32"] = _tile128(np.arange(32, dtype=f32))
    st["c_idxm32"] = _tile128(np.arange(32, dtype=f32) - np.float32(1e4))
    st["L1f_gate_wt"] = np.ascontiguousarray(np.asarray(ins["L1_ffn_gate_w"]).T)
    st["L1f_gate_bias"] = _tile128(ins["L1_ffn_gate_bias"])
    st["L1f_eg_t"] = np.ascontiguousarray(np.asarray(ins["L1_ffn_expert_gate"]).transpose(0, 2, 1))
    st["L1f_eu_t"] = np.ascontiguousarray(np.asarray(ins["L1_ffn_expert_up"]).transpose(0, 2, 1))
    st["L1f_ed_t"] = np.ascontiguousarray(np.asarray(ins["L1_ffn_expert_down"]).transpose(0, 2, 1))
    st["L1f_sg_t"] = np.ascontiguousarray(np.asarray(ins["L1_ffn_shared_gate"]).T)
    st["L1f_su_t"] = np.ascontiguousarray(np.asarray(ins["L1_ffn_shared_up"]).T)
    st["L1f_sd_t"] = np.ascontiguousarray(np.asarray(ins["L1_ffn_shared_down"]).T)
    Wq2 = np.asarray(ins["L2_attn_w_qkv"])
    st["L2_wqkv_pg"] = np.ascontiguousarray(np.stack([
        np.concatenate([Wq2[:, 128 * p:128 * p + 128],
                        Wq2[:, 1024 + 128 * p:1024 + 128 * p + 128],
                        Wq2[:, 2048 + 128 * p:2048 + 128 * p + 128]], axis=1)
        for p in range(8)]))
    st["L2_wout"] = np.ascontiguousarray(np.asarray(ins["L2_attn_w_out"]))
    wq3 = np.asarray(ins["L2_ffn_wqkv"])
    st["L2f_wqkv_pg"] = np.ascontiguousarray(np.stack([
        np.stack([np.concatenate([wq3[l, :, 128 * p:128 * p + 128],
                                  wq3[l, :, 512 + 128 * p:512 + 128 * p + 128],
                                  wq3[l, :, 1024 + 128 * p:1024 + 128 * p + 128]], axis=1)
                  for p in range(4)]) for l in range(2)]))
    st["L2f_wo"] = np.ascontiguousarray(np.asarray(ins["L2_ffn_wo"]))
    st["L2f_w1"] = np.ascontiguousarray(np.asarray(ins["L2_ffn_w1"]))
    st["L2f_w2"] = np.ascontiguousarray(np.asarray(ins["L2_ffn_w2"]))
    st["L2f_b1"] = np.ascontiguousarray(np.stack([_tile128(np.asarray(ins["L2_ffn_b1"])[l]) for l in range(2)]))
    st["L2f_b2"] = np.ascontiguousarray(np.stack([_tile128(np.asarray(ins["L2_ffn_b2"])[l]) for l in range(2)]))
    st["L2f_lng"] = np.ascontiguousarray(np.stack([_tile128(np.asarray(ins["L2_ffn_ln_g"])[l]) for l in range(2)]))
    st["L2f_lnb"] = np.ascontiguousarray(np.stack([_tile128(np.asarray(ins["L2_ffn_ln_b"])[l]) for l in range(2)]))
    st["L2f_qs"] = np.ascontiguousarray(np.stack([_tile128(np.tile(np.asarray(ins["L2_ffn_q_scale"])[l], 2)) for l in range(2)]))
    st["L2f_ks"] = np.ascontiguousarray(np.stack([_tile128(np.tile(np.asarray(ins["L2_ffn_k_scale"])[l], 2)) for l in range(2)]))
    DH = 64
    half_idx = np.arange(0, DH, 2, dtype=f32)
    inv_freq = (1.0 / np.power(np.float32(10000.0), half_idx / np.float32(DH))).astype(f32)
    tpos = np.arange(T, dtype=f32)
    freqs = np.concatenate([tpos[:, None] * inv_freq[None, :]] * 2, axis=1)
    base = ((half_idx + np.float32(0.4 * DH)) / np.float32(1.4 * DH)).astype(f32)
    power = ((tpos - np.float32(T // 2)) / np.float32(256.0)).astype(f32)
    xsc = np.concatenate([np.power(base[None, :], power[:, None]).astype(f32)] * 2, axis=1)
    cosf, sinf = np.cos(freqs).astype(f32), np.sin(freqs).astype(f32)
    def _rtile(tab):
        return np.ascontiguousarray(np.tile(tab.reshape(NT, 128, DH), (1, 1, 2)))
    st["L2f_cq"] = _rtile(cosf * xsc * np.float32(8.0))
    st["L2f_sq"] = _rtile(sinf * xsc * np.float32(8.0))
    st["L2f_ck"] = _rtile(cosf / xsc)
    st["L2f_sk"] = _rtile(sinf / xsc)
    jj = np.arange(1024)
    msk = np.zeros((4, 128, 1024), f32)
    for sdx in range(4):
        lo = 128 * sdx + np.arange(128)[:, None]
        bad = (jj[None, :] < lo) | (jj[None, :] > lo + 512)
        msk[sdx][bad] = -1e30
    st["c_l2f_mask"] = msk
    Wq3t = np.ascontiguousarray(np.asarray(ins["L3_attn_w_qkv"]).T)
    st["L3a_wqkv_pg"] = np.ascontiguousarray(np.stack([
        np.concatenate([Wq3t[:, 128 * p:128 * p + 128],
                        Wq3t[:, 1024 + 128 * p:1024 + 128 * p + 128],
                        Wq3t[:, 2048 + 128 * p:2048 + 128 * p + 128]], axis=1)
        for p in range(8)]))
    st["L3a_wproj_t"] = np.ascontiguousarray(np.asarray(ins["L3_attn_w_proj"]).T)
    st["L3a_bproj"] = _tile128(ins["L3_attn_b_proj"])
    st["L3a_wdwc"] = np.ascontiguousarray(np.asarray(ins["L3_attn_w_dwc"]).reshape(8, 128, 3).transpose(1, 0, 2))
    st["L3a_bdwc"] = np.ascontiguousarray(np.asarray(ins["L3_attn_b_dwc"]).reshape(8, 128).T)
    st["c_ones"] = np.ones((128, 128), f32)
    st["L3f_wr"] = np.ascontiguousarray(np.asarray(ins["L3_ffn_Wr"]))
    st["L3f_rbias"] = _tile128(ins["L3_ffn_router_bias"])
    st["L3f_wg"] = np.ascontiguousarray(np.asarray(ins["L3_ffn_Wg"]))
    st["L3f_wu"] = np.ascontiguousarray(np.asarray(ins["L3_ffn_Wu"]))
    st["L3f_wd"] = np.ascontiguousarray(np.asarray(ins["L3_ffn_Wd"]))
    st["L3f_convw"] = np.ascontiguousarray(np.asarray(ins["L3_ffn_conv_w"])[:, 0, :].T.reshape(8, 128, 4).transpose(1, 0, 2))
    cw4 = np.asarray(ins["L3_ffn_conv_w"])[:, 0, :]
    st["L3f_convw"] = np.ascontiguousarray(cw4.T.reshape(8, 128, 4).transpose(1, 0, 2))
    st["L3f_convb"] = np.ascontiguousarray(np.asarray(ins["L3_ffn_conv_b"]).reshape(8, 128).T)
    return st



class Prog:

    def __init__(self, nc, es):
        self.nc = nc
        self.es = es
        self.items = {e: [] for e in ENGINES}
        self.sems = {}
        self.cnt = {}
        self.sem("Q")

    def sem(self, name):
        if name not in self.sems:
            self.sems[name] = self.es.enter_context(self.nc.semaphore(name))
            self.cnt[name] = 0
        return name

    def val(self, name):
        return self.cnt[name]

    def emit(self, eng, fn, waits=(), inc=None, inc_amt=1):
        if inc is not None:
            self.sem(inc)
            self.cnt[inc] += inc_amt
        self.items[eng].append((fn, [(s, v) for s, v in waits], inc, inc_amt))

    def q(self, eng, fn):
        self.emit(eng, fn, waits=[("Q", self.val("Q"))], inc="Q", inc_amt=1)

    def qdma(self, eng, out_ap, in_ap):
        self.emit(eng, lambda e, o=out_ap, i=in_ap: e.dma_start(o, i),
                  waits=[("Q", self.val("Q"))], inc="Q", inc_amt=16)

    def qmm(self, psum_ap, pairs, tile_position=None):
        n = len(pairs)

        def fn(t, psum_ap=psum_ap, pairs=pairs, tp=tile_position):
            inst = None
            for j, (l, r) in enumerate(pairs):
                inst = t.matmul(psum_ap, l, r, start=(j == 0), stop=(j == n - 1),
                                tile_position=tp)
            return inst
        self.q("tensor", fn)

    def finish(self):
        self.emit("sync", lambda e: e.nop(), waits=[("Q", self.val("Q"))])
        with self.nc.Block() as blk:
            for eng in ENGINES:
                items = self.items[eng]
                if not items:
                    continue

                def body(e, items=items):
                    for fn, waits, inc, amt in items:
                        for s, v in waits:
                            if v > 0:
                                e.wait_ge(self.sems[s], v)
                        inst = fn(e)
                        if inc is not None:
                            assert inst is not None, "op with inc returned no instruction"
                            inst.then_inc(self.sems[inc], amt)
                getattr(blk, eng)(body)


class Ctx:
    def __init__(self, P):
        self.P = P
        self.t = {}

    def sbuf(self, name, shape, dtype):
        if name not in self.t:
            self.t[name] = self.P.es.enter_context(
                self.P.nc.sbuf_tensor(name, shape, dtype))
        return self.t[name]

    def psum(self, name, shape, dtype):
        if name not in self.t:
            self.t[name] = self.P.es.enter_context(
                self.P.nc.psum_tensor(name, shape, dtype))
        return self.t[name]

    def cs(self, val):
        return self.t["CONSTS"][:, CONST_IDX[val]:CONST_IDX[val] + 1]



def postln_tile(P, C, xt_tile, stat, scr, eps=1e-5, gamma=None, beta=None, out_ap=None):
    P.q("vector", lambda v: v.reduce_sum(stat[:, 0:1], xt_tile, axis=AX.X, negate=True))
    P.q("vector", lambda v: v.tensor_scalar(stat[:, 1:2], stat[:, 0:1], 1.0 / D, None, ALU.mult))
    P.q("scalar", lambda s: s.activation(scr, xt_tile, AF.Square,
                                         bias=stat[:, 1:2], scale=1.0,
                                         accum_out=stat[:, 2:3]))
    P.q("scalar", lambda s: s.activation(stat[:, 3:4], stat[:, 2:3], AF.Sqrt,
                                         bias=C.cs(eps), scale=1.0 / D))
    P.q("vector", lambda v: v.reciprocal(stat[:, 4:5], stat[:, 3:4]))
    if gamma is None:
        dst = xt_tile if out_ap is None else out_ap
        P.q("vector", lambda v: v.tensor_scalar(dst, xt_tile, stat[:, 1:2], stat[:, 4:5],
                                                ALU.add, ALU.mult))
    else:
        P.q("vector", lambda v: v.tensor_scalar(scr, xt_tile, stat[:, 1:2], stat[:, 4:5],
                                                ALU.add, ALU.mult))
        P.q("vector", lambda v: v.tensor_tensor(scr, scr, gamma, ALU.mult))
        P.q("gpsimd", lambda g: g.tensor_tensor(out_ap, scr, beta, ALU.add))


def tok_to_ch(P, C, src_bf_tile, dst_ap, nd=ND):
    pb = C.psum("PBtr", [128, 1024], BF16)

    def trs(t):
        inst = None
        for k in range(nd):
            inst = t.transpose(pb[:, k * 128:(k + 1) * 128],
                               src_bf_tile[:, k * 128:(k + 1) * 128], ident_of(C))
        return inst
    P.q("tensor", trs)
    P.q("vector", lambda v: v.tensor_copy(dst_ap, pb[:, 0:nd * 128]))


def ident_of(C):
    return C.t["ident"][:, :]



def build_L0a(nc, es):
    P = Prog(nc, es)
    C = Ctx(P)
    x_in = nc.dram_tensor("x_in", [S_PER_CORE, T, D], F32, kind="ExternalInput").ap()
    x_out = nc.dram_tensor("x_out", [S_PER_CORE, T, D], F32, kind="ExternalOutput").ap()
    w = {k: nc.dram_tensor(k, list(v.shape), F32, kind="ExternalInput").ap()
         for k, v in (("c_scalars", np.zeros((128, len(CONST_VALS)))),
                      ("c_ident", np.zeros((128, 128))),
                      ("c_causal0", np.zeros((128, 128))),
                      ("L0_wqkv_pg", np.zeros((8, 1024, 384))),
                      ("L0_wo_t", np.zeros((1024, 1024))),
                      ("L0_gamma", np.zeros((128, D))),
                      ("L0_beta", np.zeros((128, D))))}

    CONSTS = C.sbuf("CONSTS", [128, len(CONST_VALS)], F32)
    ident = C.sbuf("ident", [128, 128], BF16)
    causal = C.sbuf("causal", [128, 128], F32)
    gamma = C.sbuf("gamma", [128, D], F32)
    beta = C.sbuf("beta", [128, D], F32)
    XT = C.sbuf("XT", [128, NT, D], F32)
    XN = C.sbuf("XN", [128, D], BF16)
    XN_C = C.sbuf("XN_C", [128, ND, T], BF16)
    WP = C.sbuf("WP", [128, ND, 384], BF16)
    WO = C.sbuf("WO", [128, ND, D], BF16)
    QPC = C.sbuf("QPC", [128, T], BF16)
    KPC = C.sbuf("KPC", [128, T], BF16)
    VPT = C.sbuf("VPT", [128, NT, 128], BF16)
    SS = C.sbuf("SS", [128, T], F32)
    PP = C.sbuf("PP", [128, T], BF16)
    PT = C.sbuf("PT", [128, T], BF16)
    CTX = C.sbuf("CTX", [128, ND, T], BF16)
    STAT = C.sbuf("STAT", [128, 64], F32)
    SCR = C.sbuf("SCR", [128, D], F32)

    PA = C.psum("PA", [128, 512], F32)
    PB2 = C.psum("PB2", [128, 512], F32)
    PC = C.psum("PC", [128, 128], F32)

    P.qdma("gpsimd", CONSTS[:], w["c_scalars"])
    P.qdma("gpsimd", ident[:], w["c_ident"])
    P.qdma("gpsimd", causal[:], w["c_causal0"])
    P.qdma("gpsimd", gamma[:], w["L0_gamma"])
    P.qdma("gpsimd", beta[:], w["L0_beta"])
    P.qdma("gpsimd", WO[:, :, :], w["L0_wo_t"].rearrange("(b p) n -> p b n", p=128))

    for s in range(S_PER_CORE):
        P.qdma("sync", XT[:, :, :], x_in[s].rearrange("(i p) d -> p i d", p=128))

        for i in range(NT):
            postln_tile(P, C, XT[:, i, :], STAT, SCR[:, :], eps=1e-5,
                        gamma=gamma[:], beta=beta[:], out_ap=XN[:])
            tok_to_ch(P, C, XN[:], XN_C[:, :, i * 128:(i + 1) * 128])

        for p in range(8):
            P.qdma("gpsimd", WP[:, :, :], w["L0_wqkv_pg"][p].rearrange("(b p) n -> p b n", p=128))
            for dst, col0 in ((QPC, 0), (KPC, 128)):
                for c in range(4):
                    P.qmm(PA[:, 0:512],
                          [(WP[:, k, col0:col0 + 128], XN_C[:, k, c * 512:(c + 1) * 512])
                           for k in range(ND)])
                    P.q("vector", lambda v, dst=dst, c=c: v.tensor_copy(
                        dst[:, c * 512:(c + 1) * 512], PA[:, 0:512]))
            for i in range(NT):
                P.qmm(PC[:, 0:128],
                      [(XN_C[:, k, i * 128:(i + 1) * 128], WP[:, k, 256:384])
                       for k in range(ND)])
                P.q("scalar", lambda sc, i=i: sc.copy(VPT[:, i, :], PC[:, 0:128]))

            for hh in range(2):
                qsl = slice(64 * hh, 64 * hh + 64)
                for i in range(NT):
                    nkt = i + 1
                    nk = nkt * 128
                    for c in range((nk + 511) // 512):
                        k0, k1 = c * 512, min(nk, (c + 1) * 512)
                        pa = PA if c % 2 == 0 else PB2
                        P.qmm(pa[:, 0:k1 - k0],
                              [(QPC[qsl, i * 128:(i + 1) * 128], KPC[qsl, k0:k1])])
                        if k1 == nk:
                            if k1 - k0 > 128:
                                P.q("vector", lambda v, k0=k0, k1=k1, pa=pa: v.tensor_copy(
                                    SS[:, k0:k1 - 128], pa[:, 0:k1 - k0 - 128]))
                            P.q("vector", lambda v, k0=k0, k1=k1, pa=pa: v.tensor_tensor(
                                SS[:, k1 - 128:k1], pa[:, k1 - k0 - 128:k1 - k0],
                                causal[:], ALU.add))
                        else:
                            P.q("vector", lambda v, k0=k0, k1=k1, pa=pa: v.tensor_copy(
                                SS[:, k0:k1], pa[:, 0:k1 - k0]))
                    P.q("vector", lambda v, nk=nk: v.reduce_max(
                        STAT[:, 8:9], SS[:, 0:nk], axis=AX.X, negate=True))
                    P.q("vector", lambda v: v.tensor_scalar(
                        STAT[:, 9:10], STAT[:, 8:9], 0.125, None, ALU.mult))
                    P.q("scalar", lambda sc, nk=nk: sc.activation(
                        PP[:, 0:nk], SS[:, 0:nk], AF.Exp, bias=STAT[:, 9:10], scale=0.125,
                        accum_out=STAT[:, 10:11]))
                    P.q("vector", lambda v: v.reciprocal(STAT[:, 11:12], STAT[:, 10:11]))
                    P.q("vector", lambda v, nk=nk: v.tensor_scalar(
                        PP[:, 0:nk], PP[:, 0:nk], STAT[:, 11:12], None, ALU.mult))
                    for r0 in range(0, nkt, 8):
                        r1 = min(nkt, r0 + 8)
                        pbt = C.psum("PBtr", [128, 1024], BF16)

                        def ptr(t, r0=r0, r1=r1, pbt=pbt):
                            inst = None
                            for j in range(r0, r1):
                                inst = t.transpose(pbt[:, (j - r0) * 128:(j - r0 + 1) * 128],
                                                   PP[:, j * 128:(j + 1) * 128], ident[:])
                            return inst
                        P.q("tensor", ptr)
                        P.q("vector", lambda v, r0=r0, r1=r1, pbt=pbt: v.tensor_copy(
                            PT[:, r0 * 128:r1 * 128], pbt[:, 0:(r1 - r0) * 128]))
                    tp = (0, 64 * hh)
                    P.qmm(PC[64 * hh:64 * hh + 64, 0:128],
                          [(VPT[:, j, qsl], PT[:, j * 128:(j + 1) * 128]) for j in range(nkt)],
                          tile_position=tp)
                    P.q("scalar", lambda sc, p=p, i=i, hh=hh: sc.copy(
                        CTX[64 * hh:64 * hh + 64, p, i * 128:(i + 1) * 128],
                        PC[64 * hh:64 * hh + 64, 0:128]))

        for i in range(NT):
            for n in range(2):
                P.qmm(PA[:, 0:512],
                      [(CTX[:, k, i * 128:(i + 1) * 128], WO[:, k, n * 512:(n + 1) * 512])
                       for k in range(ND)])
                P.q("vector", lambda v, i=i, n=n: v.tensor_tensor(
                    XT[:, i, n * 512:(n + 1) * 512], PA[:, 0:512],
                    XT[:, i, n * 512:(n + 1) * 512], ALU.add))
            postln_tile(P, C, XT[:, i, :], STAT, SCR[:, :], eps=1e-5)
        P.qdma("sync", x_out[s].rearrange("(i p) d -> p i d", p=128), XT[:, :, :])

    P.finish()
    return ["x_in", "c_scalars", "c_ident", "c_causal0", "L0_wqkv_pg", "L0_wo_t", "L0_gamma", "L0_beta"]


def build_L0f(nc, es):
    P = Prog(nc, es)
    C = Ctx(P)
    x_in = nc.dram_tensor("x_in", [S_PER_CORE, T, D], F32, kind="ExternalInput").ap()
    x_out = nc.dram_tensor("x_out", [S_PER_CORE, T, D], F32, kind="ExternalOutput").ap()
    names = ["x_in", "c_scalars", "c_ident", "L0f_w_t", "L0f_scale", "L0f_bias"]
    shp = {"c_scalars": [128, len(CONST_VALS)], "c_ident": [128, 128],
           "L0f_w_t": [D, D], "L0f_scale": [128, D], "L0f_bias": [128, D]}
    w = {k: nc.dram_tensor(k, shp[k], F32, kind="ExternalInput").ap() for k in names[1:]}

    CONSTS = C.sbuf("CONSTS", [128, len(CONST_VALS)], F32)
    ident = C.sbuf("ident", [128, 128], BF16)
    scale = C.sbuf("scale", [128, D], F32)
    bias = C.sbuf("bias", [128, D], F32)
    XT = C.sbuf("XT", [128, NT, D], F32)
    XB = C.sbuf("XB", [128, D], BF16)
    XC = C.sbuf("XC", [128, ND, T], BF16)
    SW = C.sbuf("SW", [128, ND, D], BF16)
    STAT = C.sbuf("STAT", [128, 64], F32)
    SCR = C.sbuf("SCR", [128, D], F32)
    PA = C.psum("PA", [128, 512], F32)

    P.qdma("gpsimd", CONSTS[:], w["c_scalars"])
    P.qdma("gpsimd", ident[:], w["c_ident"])
    P.qdma("gpsimd", scale[:], w["L0f_scale"])
    P.qdma("gpsimd", bias[:], w["L0f_bias"])
    P.qdma("gpsimd", SW[:, :, :], w["L0f_w_t"].rearrange("(b p) n -> p b n", p=128))
    P.q("vector", lambda v: v.tensor_scalar(SW.ap().rearrange("p a b -> p (a b)"),
                                            SW.ap().rearrange("p a b -> p (a b)"),
                                            0.0, None, ALU.is_ge))
    P.q("vector", lambda v: v.tensor_scalar(SW.ap().rearrange("p a b -> p (a b)"),
                                            SW.ap().rearrange("p a b -> p (a b)"),
                                            2.0, 1.0, ALU.mult, ALU.subtract))

    for s in range(S_PER_CORE):
        P.qdma("sync", XT[:, :, :], x_in[s].rearrange("(i p) d -> p i d", p=128))
        for i in range(NT):
            P.q("vector", lambda v, i=i: v.tensor_copy(XB[:], XT[:, i, :]))
            tok_to_ch(P, C, XB[:], XC[:, :, i * 128:(i + 1) * 128])
        for i in range(NT):
            for n in range(2):
                P.qmm(PA[:, 0:512],
                      [(XC[:, k, i * 128:(i + 1) * 128], SW[:, k, n * 512:(n + 1) * 512])
                       for k in range(ND)])
                sl = slice(n * 512, (n + 1) * 512)
                P.q("vector", lambda v, i=i, sl=sl: v.tensor_tensor(
                    SCR[:, sl], PA[:, 0:512], scale[:, sl], ALU.mult))
                P.q("vector", lambda v, i=i, sl=sl: v.tensor_tensor(
                    SCR[:, sl], SCR[:, sl], bias[:, sl], ALU.add))
                P.q("vector", lambda v, i=i, sl=sl: v.tensor_scalar(
                    SCR[:, sl], SCR[:, sl], 100.0, -100.0, ALU.min, ALU.max))
                P.q("vector", lambda v, i=i, sl=sl: v.tensor_tensor(
                    XT[:, i, sl], XT[:, i, sl], SCR[:, sl], ALU.add))
            postln_tile(P, C, XT[:, i, :], STAT, SCR[:, :], eps=1e-5)
        P.qdma("sync", x_out[s].rearrange("(i p) d -> p i d", p=128), XT[:, :, :])
    P.finish()
    return names


def build_L1a(nc, es):
    P = Prog(nc, es)
    C = Ctx(P)
    x_in = nc.dram_tensor("x_in", [S_PER_CORE, T, D], F32, kind="ExternalInput").ap()
    x_out = nc.dram_tensor("x_out", [S_PER_CORE, T, D], F32, kind="ExternalOutput").ap()
    names = ["x_in", "c_scalars", "c_ident", "c_strict_keep", "c_strict_fill",
             "L1_wqkv_pg", "L1_wo", "L1_w1", "L1_w2", "L1_ln1", "L1_ln2",
             "L1_scaleh", "L1_btiles"]
    shp = {"c_scalars": [128, len(CONST_VALS)], "c_ident": [128, 128],
           "c_strict_keep": [128, 128], "c_strict_fill": [128, 128],
           "L1_wqkv_pg": [8, D, 384], "L1_wo": [D, D], "L1_w1": [D, 4096],
           "L1_w2": [4096, D], "L1_ln1": [128, D], "L1_ln2": [128, D],
           "L1_scaleh": [128, 16], "L1_btiles": [16, 16, 128, 128]}
    w = {k: nc.dram_tensor(k, shp[k], F32, kind="ExternalInput").ap() for k in names[1:]}

    CONSTS = C.sbuf("CONSTS", [128, len(CONST_VALS)], F32)
    ident = C.sbuf("ident", [128, 128], BF16)
    mkeep = C.sbuf("mkeep", [128, 128], F32)
    mfill = C.sbuf("mfill", [128, 128], F32)
    ln1 = C.sbuf("ln1", [128, D], BF16)
    ln2 = C.sbuf("ln2", [128, D], BF16)
    scaleh = C.sbuf("scaleh", [128, 16], F32)
    XT = C.sbuf("XT", [128, NT, D], F32)
    XB = C.sbuf("XB", [128, D], BF16)
    STAT = C.sbuf("STAT", [128, 64], F32)
    SCR = C.sbuf("SCR", [128, D], F32)
    ARE = C.sbuf("ARE", [128, 128 * 512], BF16)

    def V(off_kb, shape, dtype=BF16):
        esz = 2 if dtype == BF16 else 4
        n = int(np.prod(shape[1:]))
        ap = ARE.ap()[:, off_kb * 512: off_kb * 512 + (n * esz) // 2]
        if dtype == F32:
            ap = ap.bitcast(F32)
        if len(shape) == 3:
            ap = ap.rearrange("p (a b) -> p a b", a=shape[1])
        return ap

    XN_C = V(0, [128, ND, T])
    CTX = V(32, [128, ND, T])
    WP = V(64, [128, ND, 384])
    QTP = V(70, [128, NT, 128])
    KTP = V(74, [128, NT, 128])
    VPT = V(78, [128, NT, 128])
    QPC = V(82, [128, T])
    KPC = V(86, [128, T])
    PP = V(90, [128, T])
    PT = V(94, [128, T])
    BH = V(98, [128, NT, 128], F32)
    SS = V(106, [128, T], F32)
    YT = V(0, [128, NT, D])
    WO = V(64, [128, ND, D])
    ZC = V(32, [128, ND, T])
    W1Q = V(64, [128, ND, 1024])
    W2Q = V(80, [128, ND, 1024])
    H1 = V(96, [128, 1024])
    H1C = V(100, [128, ND, 128])

    PA = C.psum("PA", [128, 512], F32)
    PB2 = C.psum("PB2", [128, 512], F32)
    PC = C.psum("PC", [128, 128], F32)

    P.qdma("gpsimd", CONSTS[:], w["c_scalars"])
    P.qdma("gpsimd", ident[:], w["c_ident"])
    P.qdma("gpsimd", mkeep[:], w["c_strict_keep"])
    P.qdma("gpsimd", mfill[:], w["c_strict_fill"])
    P.qdma("gpsimd", ln1[:], w["L1_ln1"])
    P.qdma("gpsimd", ln2[:], w["L1_ln2"])
    P.qdma("gpsimd", scaleh[:], w["L1_scaleh"])

    for s in range(S_PER_CORE):
        P.qdma("sync", XT[:, :, :], x_in[s].rearrange("(i p) d -> p i d", p=128))

        for i in range(NT):
            P.q("scalar", lambda sc, i=i: sc.activation(SCR[:], XT[:, i, :], AF.Square,
                                                        accum_out=STAT[:, 0:1]))
            P.q("scalar", lambda sc: sc.activation(STAT[:, 1:2], STAT[:, 0:1], AF.Sqrt,
                                                   bias=C.cs(1e-6), scale=1.0 / D))
            P.q("vector", lambda v: v.reciprocal(STAT[:, 2:3], STAT[:, 1:2]))
            P.q("vector", lambda v, i=i: v.tensor_scalar(SCR[:], XT[:, i, :],
                                                         STAT[:, 2:3], None, ALU.mult))
            P.q("vector", lambda v: v.tensor_tensor(XB[:], SCR[:], ln1[:], ALU.mult))
            tok_to_ch(P, C, XB[:], XN_C[:, :, i * 128:(i + 1) * 128])

        for p in range(8):
            P.qdma("gpsimd", WP, w["L1_wqkv_pg"][p].rearrange("(b p) n -> p b n", p=128))
            for which, dstT, col0 in (("q", QTP, 0), ("k", KTP, 128), ("v", VPT, 256)):
                for i in range(NT):
                    P.qmm(PC[:, 0:128],
                          [(XN_C[:, k, i * 128:(i + 1) * 128], WP[:, k, col0:col0 + 128])
                           for k in range(ND)])
                    if which == "v":
                        P.q("scalar", lambda sc, i=i, dstT=dstT: sc.copy(
                            dstT[:, i, :], PC[:, 0:128]))
                    else:
                        for hh in range(2):
                            hs = slice(64 * hh, 64 * hh + 64)
                            P.q("scalar", lambda sc, hs=hs: sc.activation(
                                SCR[:, 0:64], PC[:, hs], AF.Square, accum_out=STAT[:, 4:5]))
                            P.q("scalar", lambda sc: sc.activation(
                                STAT[:, 5:6], STAT[:, 4:5], AF.Sqrt, bias=C.cs(1e-6), scale=1.0))
                            P.q("vector", lambda v: v.reciprocal(STAT[:, 6:7], STAT[:, 5:6]))
                            P.q("vector", lambda v, i=i, hs=hs, dstT=dstT: v.tensor_scalar(
                                dstT[:, i, hs], PC[:, hs], STAT[:, 6:7], None, ALU.mult))
            for i in range(NT):
                pbt = C.psum("PBtr", [128, 1024], BF16)
                P.q("tensor", lambda t, i=i, pbt=pbt: t.transpose(pbt[:, 0:128], QTP[:, i, :], ident[:]))
                P.q("vector", lambda v, i=i, pbt=pbt: v.tensor_copy(QPC[:, i * 128:(i + 1) * 128],
                                                                    pbt[:, 0:128]))
                P.q("tensor", lambda t, i=i, pbt=pbt: t.transpose(pbt[:, 128:256], KTP[:, i, :], ident[:]))
                P.q("vector", lambda v, i=i, pbt=pbt: v.tensor_copy(KPC[:, i * 128:(i + 1) * 128],
                                                                    pbt[:, 128:256]))

            for hh in range(2):
                h = 2 * p + hh
                P.qdma("gpsimd", BH, w["L1_btiles"][h].rearrange("t p b -> p t b"))
                qsl = slice(64 * hh, 64 * hh + 64)
                for i in range(NT):
                    nkt_c = i + 1
                    nk = 2048 if i == 0 else nkt_c * 128
                    nkt_av = 16 if i == 0 else nkt_c
                    for c in range((nkt_c * 128 + 511) // 512):
                        k0, k1 = c * 512, min(nkt_c * 128, (c + 1) * 512)
                        pa = PA if c % 2 == 0 else PB2
                        P.qmm(pa[:, 0:k1 - k0],
                              [(QPC[qsl, i * 128:(i + 1) * 128], KPC[qsl, k0:k1])])
                        for j in range(k0 // 128, k1 // 128):
                            P.q("vector", lambda v, j=j, k0=k0, pa=pa, i=i: v.tensor_tensor(
                                SS[:, j * 128:(j + 1) * 128],
                                pa[:, j * 128 - k0:(j + 1) * 128 - k0],
                                BH[:, i - j, :], ALU.add))
                    P.q("vector", lambda v, nk2=nkt_c * 128, h=h: v.tensor_scalar(
                        SS[:, 0:nk2], SS[:, 0:nk2], scaleh[:, h:h + 1], None, ALU.mult))
                    dof = (nkt_c - 1) * 128
                    P.q("vector", lambda v, dof=dof: v.tensor_tensor(
                        SS[:, dof:dof + 128], SS[:, dof:dof + 128], mkeep[:], ALU.mult))
                    P.q("vector", lambda v, dof=dof: v.tensor_tensor(
                        SS[:, dof:dof + 128], SS[:, dof:dof + 128], mfill[:], ALU.add))
                    if i == 0:
                        P.q("gpsimd", lambda g: g.memset(SS[:, 128:2048], -1e6))
                    P.q("vector", lambda v, nk=nk: v.reduce_max(
                        STAT[:, 8:9], SS[:, 0:nk], axis=AX.X, negate=True))
                    P.q("scalar", lambda sc, nk=nk: sc.activation(
                        PP[:, 0:nk], SS[:, 0:nk], AF.Exp, bias=STAT[:, 8:9], scale=1.0,
                        accum_out=STAT[:, 10:11]))
                    P.q("vector", lambda v: v.reciprocal(STAT[:, 11:12], STAT[:, 10:11]))
                    P.q("vector", lambda v, nk=nk: v.tensor_scalar(
                        PP[:, 0:nk], PP[:, 0:nk], STAT[:, 11:12], None, ALU.mult))
                    for r0 in range(0, nkt_av, 8):
                        r1 = min(nkt_av, r0 + 8)
                        pbt = C.psum("PBtr", [128, 1024], BF16)

                        def ptr(t, r0=r0, r1=r1, pbt=pbt):
                            inst = None
                            for j in range(r0, r1):
                                inst = t.transpose(pbt[:, (j - r0) * 128:(j - r0 + 1) * 128],
                                                   PP[:, j * 128:(j + 1) * 128], ident[:])
                            return inst
                        P.q("tensor", ptr)
                        P.q("vector", lambda v, r0=r0, r1=r1, pbt=pbt: v.tensor_copy(
                            PT[:, r0 * 128:r1 * 128], pbt[:, 0:(r1 - r0) * 128]))
                    P.qmm(PC[64 * hh:64 * hh + 64, 0:128],
                          [(VPT[:, j, qsl], PT[:, j * 128:(j + 1) * 128]) for j in range(nkt_av)],
                          tile_position=(0, 64 * hh))
                    P.q("scalar", lambda sc, p=p, i=i, hh=hh: sc.copy(
                        CTX[64 * hh:64 * hh + 64, p, i * 128:(i + 1) * 128],
                        PC[64 * hh:64 * hh + 64, 0:128]))

        P.qdma("gpsimd", WO, w["L1_wo"].rearrange("(b p) n -> p b n", p=128))
        for i in range(NT):
            for n in range(2):
                P.qmm(PA[:, 0:512],
                      [(CTX[:, k, i * 128:(i + 1) * 128], WO[:, k, n * 512:(n + 1) * 512])
                       for k in range(ND)])
                sl = slice(n * 512, (n + 1) * 512)
                P.q("vector", lambda v, i=i, sl=sl: v.tensor_tensor(
                    YT[:, i, sl], PA[:, 0:512], XT[:, i, sl], ALU.add))
                P.q("vector", lambda v, i=i, sl=sl: v.tensor_tensor(
                    XT[:, i, sl], XT[:, i, sl], YT[:, i, sl], ALU.add))

        for i in range(NT):
            P.q("scalar", lambda sc, i=i: sc.activation(SCR[:], YT[:, i, :], AF.Square,
                                                        accum_out=STAT[:, 0:1]))
            P.q("scalar", lambda sc: sc.activation(STAT[:, 1:2], STAT[:, 0:1], AF.Sqrt,
                                                   bias=C.cs(1e-6), scale=1.0 / D))
            P.q("vector", lambda v: v.reciprocal(STAT[:, 2:3], STAT[:, 1:2]))
            P.q("vector", lambda v, i=i: v.tensor_scalar(SCR[:], YT[:, i, :],
                                                         STAT[:, 2:3], None, ALU.mult))
            P.q("vector", lambda v: v.tensor_tensor(XB[:], SCR[:], ln2[:], ALU.mult))
            tok_to_ch(P, C, XB[:], ZC[:, :, i * 128:(i + 1) * 128])

        for qtr in range(4):
            P.qdma("gpsimd", W1Q,
                   w["L1_w1"][:, qtr * 1024:(qtr + 1) * 1024].rearrange("(b p) n -> p b n", p=128))
            P.qdma("gpsimd", W2Q,
                   w["L1_w2"][qtr * 1024:(qtr + 1) * 1024, :].rearrange("(b p) n -> p b n", p=128))
            for i in range(NT):
                for c in range(2):
                    P.qmm(PA[:, 0:512],
                          [(ZC[:, k, i * 128:(i + 1) * 128], W1Q[:, k, c * 512:(c + 1) * 512])
                           for k in range(ND)])
                    P.q("scalar", lambda sc, c=c: sc.activation(
                        H1[:, c * 512:(c + 1) * 512], PA[:, 0:512], AF.Relu))
                pbt = C.psum("PBtr", [128, 1024], BF16)

                def htr(t, pbt=pbt):
                    inst = None
                    for j in range(8):
                        inst = t.transpose(pbt[:, j * 128:(j + 1) * 128],
                                           H1[:, j * 128:(j + 1) * 128], ident[:])
                    return inst
                P.q("tensor", htr)
                P.q("vector", lambda v, pbt=pbt: v.tensor_copy(
                    H1C[:, :, :], pbt[:, 0:1024]))
                for n in range(2):
                    P.qmm(PA[:, 0:512],
                          [(H1C[:, kb, :], W2Q[:, kb, n * 512:(n + 1) * 512])
                           for kb in range(ND)])
                    sl = slice(n * 512, (n + 1) * 512)
                    P.q("vector", lambda v, i=i, sl=sl: v.tensor_tensor(
                        XT[:, i, sl], XT[:, i, sl], PA[:, 0:512], ALU.add))

        for i in range(NT):
            postln_tile(P, C, XT[:, i, :], STAT, SCR[:, :], eps=1e-5)
        P.qdma("sync", x_out[s].rearrange("(i p) d -> p i d", p=128), XT[:, :, :])
    P.finish()
    return names


def argmax_first(P, vals_ap, iot, idxm, eq_out, tmp_a, tmp_b):
    P.q("vector", lambda v: v.reduce_max(tmp_b[:, 0:1], vals_ap, axis=AX.X))
    P.q("vector", lambda v: v.tensor_scalar(tmp_a, vals_ap, tmp_b[:, 0:1], None, ALU.is_equal))
    P.q("vector", lambda v: v.tensor_tensor(tmp_a, tmp_a, idxm, ALU.mult))
    P.q("vector", lambda v: v.tensor_reduce(tmp_b[:, 1:2], tmp_a, axis=AX.X, op=ALU.min))
    P.q("vector", lambda v: v.tensor_scalar(tmp_b[:, 1:2], tmp_b[:, 1:2], 1e4, None, ALU.add))
    P.q("vector", lambda v: v.tensor_scalar(eq_out, iot, tmp_b[:, 1:2], None, ALU.is_equal))


def build_L1f(nc, es):
    P = Prog(nc, es)
    C = Ctx(P)
    x_in = nc.dram_tensor("x_in", [S_PER_CORE, T, D], F32, kind="ExternalInput").ap()
    x_out = nc.dram_tensor("x_out", [S_PER_CORE, T, D], F32, kind="ExternalOutput").ap()
    names = ["x_in", "c_scalars", "c_ident", "c_iota32", "c_idxm32",
             "L1f_gate_wt", "L1f_gate_bias", "L1f_eg_t", "L1f_eu_t", "L1f_ed_t",
             "L1f_sg_t", "L1f_su_t", "L1f_sd_t"]
    shp = {"c_scalars": [128, len(CONST_VALS)], "c_ident": [128, 128],
           "c_iota32": [128, 32], "c_idxm32": [128, 32],
           "L1f_gate_wt": [D, 32], "L1f_gate_bias": [128, 32],
           "L1f_eg_t": [32, D, 512], "L1f_eu_t": [32, D, 512], "L1f_ed_t": [32, 512, D],
           "L1f_sg_t": [D, D], "L1f_su_t": [D, D], "L1f_sd_t": [D, D]}
    w = {k: nc.dram_tensor(k, shp[k], F32, kind="ExternalInput").ap() for k in names[1:]}

    CONSTS = C.sbuf("CONSTS", [128, len(CONST_VALS)], F32)
    ident = C.sbuf("ident", [128, 128], BF16)
    iota32 = C.sbuf("iota32", [128, 32], F32)
    idxm32 = C.sbuf("idxm32", [128, 32], F32)
    gwt = C.sbuf("gwt", [128, ND, 32], BF16)
    gbias = C.sbuf("gbias", [128, 32], F32)
    XT = C.sbuf("XT", [128, NT, D], F32)
    XB = C.sbuf("XB", [128, D], BF16)
    XC = C.sbuf("XC", [128, ND, T], BF16)
    ACC = C.sbuf("ACC", [128, NT, D], F32)
    CW = C.sbuf("CW", [128, NT, 32], F32)
    RT = C.sbuf("RT", [128, 256], F32)
    WG = C.sbuf("WG", [128, ND, 512], BF16)
    WU = C.sbuf("WU", [128, ND, 512], BF16)
    WD = C.sbuf("WD", [128, 4, D], BF16)
    HG = C.sbuf("HG", [128, 512], BF16)
    HID = C.sbuf("HID", [128, 512], BF16)
    HIDC = C.sbuf("HIDC", [128, 4, 128], BF16)
    STAT = C.sbuf("STAT", [128, 64], F32)
    SCR = C.sbuf("SCR", [128, D], F32)
    PA = C.psum("PA", [128, 512], F32)
    PB2 = C.psum("PB2", [128, 512], F32)

    P.qdma("gpsimd", CONSTS[:], w["c_scalars"])
    P.qdma("gpsimd", ident[:], w["c_ident"])
    P.qdma("gpsimd", iota32[:], w["c_iota32"])
    P.qdma("gpsimd", idxm32[:], w["c_idxm32"])
    P.qdma("gpsimd", gwt[:, :, :], w["L1f_gate_wt"].rearrange("(b p) n -> p b n", p=128))
    P.qdma("gpsimd", gbias[:], w["L1f_gate_bias"])

    for s in range(S_PER_CORE):
        P.qdma("sync", XT[:, :, :], x_in[s].rearrange("(i p) d -> p i d", p=128))
        for i in range(NT):
            P.q("vector", lambda v, i=i: v.tensor_copy(XB[:], XT[:, i, :]))
            tok_to_ch(P, C, XB[:], XC[:, :, i * 128:(i + 1) * 128])

        SC = RT[:, 0:32]; CH = RT[:, 32:64]; CUR = RT[:, 64:96]; SELM = RT[:, 96:128]
        EQ = RT[:, 128:160]; TMP = RT[:, 160:192]; TB = RT[:, 192:200]
        GS = RT[:, 200:204]; GT2 = RT[:, 204:208]; GSUM = RT[:, 208:212]
        GEQ = RT[:, 212:216]; GMASK = RT[:, 216:220]
        for i in range(NT):
            P.qmm(PA[:, 0:32], [(XC[:, k, i * 128:(i + 1) * 128], gwt[:, k, :])
                                for k in range(ND)])
            P.q("scalar", lambda sc: sc.activation(SC, PA[:, 0:32], AF.Sigmoid))
            P.q("vector", lambda v: v.tensor_tensor(CH, SC, gbias[:], ALU.add))
            for g in range(4):
                sl = slice(8 * g, 8 * g + 8)
                argmax_first(P, RT[:, 32 + 8 * g:40 + 8 * g],
                             iota32[:, 0:8], idxm32[:, 0:8], EQ[:, 0:8], TMP[:, 0:8], TB)
                P.q("vector", lambda v, g=g: v.tensor_copy(GS[:, g:g + 1], TB[:, 0:1]))
                P.q("vector", lambda v, g=g: v.scalar_tensor_tensor(
                    TMP[:, 0:8], EQ[:, 0:8], -1e30, RT[:, 32 + 8 * g:40 + 8 * g],
                    ALU.mult, ALU.add))
                P.q("vector", lambda v, g=g: v.reduce_max(GT2[:, g:g + 1], TMP[:, 0:8], axis=AX.X))
            P.q("vector", lambda v: v.tensor_tensor(GSUM, GS, GT2, ALU.add))
            argmax_first(P, GSUM, iota32[:, 0:4], idxm32[:, 0:4], GEQ, TMP[:, 0:4], TB)
            P.q("vector", lambda v: v.tensor_copy(GMASK, GEQ))
            P.q("vector", lambda v: v.scalar_tensor_tensor(
                TMP[:, 0:4], GEQ, -1e30, GSUM, ALU.mult, ALU.add))
            argmax_first(P, TMP[:, 0:4], iota32[:, 0:4], idxm32[:, 0:4], GEQ, TMP[:, 4:8], TB)
            P.q("vector", lambda v: v.tensor_tensor(GMASK, GMASK, GEQ, ALU.add))
            gmask_b = bass.AP(RT, 216, [[256, 128], [1, 4], [0, 8]])
            P.q("vector", lambda v: v.tensor_tensor(
                CUR.rearrange("p (a b) -> p a b", a=4), CH.rearrange("p (a b) -> p a b", a=4),
                gmask_b, ALU.mult))
            for it in range(8):
                argmax_first(P, CUR, iota32[:], idxm32[:], EQ, TMP, TB)
                if it == 0:
                    P.q("vector", lambda v: v.tensor_copy(SELM, EQ))
                else:
                    P.q("vector", lambda v: v.tensor_tensor(SELM, SELM, EQ, ALU.add))
                if it < 7:
                    P.q("vector", lambda v: v.scalar_tensor_tensor(
                        CUR, EQ, -1e30, CUR, ALU.mult, ALU.add))
            P.q("vector", lambda v: v.tensor_tensor(TMP, SELM, SC, ALU.mult))
            P.q("vector", lambda v: v.reduce_sum(TB[:, 2:3], TMP, axis=AX.X))
            P.q("vector", lambda v: v.tensor_scalar(TB[:, 2:3], TB[:, 2:3], 1e-20, None, ALU.add))
            P.q("vector", lambda v: v.reciprocal(TB[:, 3:4], TB[:, 2:3]))
            P.q("vector", lambda v: v.tensor_scalar(TB[:, 3:4], TB[:, 3:4], 2.5, None, ALU.mult))
            P.q("vector", lambda v, i=i: v.tensor_scalar(CW[:, i, :], TMP, TB[:, 3:4], None, ALU.mult))

        for e in range(32):
            P.qdma("gpsimd", WG[:, :, :], w["L1f_eg_t"][e].rearrange("(b p) n -> p b n", p=128))
            P.qdma("gpsimd", WU[:, :, :], w["L1f_eu_t"][e].rearrange("(b p) n -> p b n", p=128))
            P.qdma("gpsimd", WD[:, :, :], w["L1f_ed_t"][e].rearrange("(b p) n -> p b n", p=128))
            for i in range(NT):
                P.qmm(PA[:, 0:512], [(XC[:, k, i * 128:(i + 1) * 128], WG[:, k, :])
                                     for k in range(ND)])
                P.q("scalar", lambda sc: sc.activation(HG[:], PA[:, 0:512], AF.Silu))
                P.qmm(PB2[:, 0:512], [(XC[:, k, i * 128:(i + 1) * 128], WU[:, k, :])
                                      for k in range(ND)])
                P.q("vector", lambda v: v.tensor_tensor(HID[:], PB2[:, 0:512], HG[:], ALU.mult))
                pbt = C.psum("PBtr", [128, 1024], BF16)

                def htr(t, pbt=pbt):
                    inst = None
                    for j in range(4):
                        inst = t.transpose(pbt[:, j * 128:(j + 1) * 128],
                                           HID[:, j * 128:(j + 1) * 128], ident[:])
                    return inst
                P.q("tensor", htr)
                P.q("vector", lambda v, pbt=pbt: v.tensor_copy(HIDC[:, :, :], pbt[:, 0:512]))
                for n in range(2):
                    P.qmm(PA[:, 0:512], [(HIDC[:, kb, :], WD[:, kb, n * 512:(n + 1) * 512])
                                         for kb in range(4)])
                    sl = slice(n * 512, (n + 1) * 512)
                    if e == 0:
                        P.q("vector", lambda v, i=i, sl=sl, e=e: v.tensor_scalar(
                            ACC[:, i, sl], PA[:, 0:512], CW[:, i, e:e + 1], None, ALU.mult))
                    else:
                        P.q("vector", lambda v, i=i, sl=sl, e=e: v.scalar_tensor_tensor(
                            ACC[:, i, sl], PA[:, 0:512], CW[:, i, e:e + 1],
                            ACC[:, i, sl], ALU.mult, ALU.add))

        for half in range(2):
            c0 = half * 512
            P.qdma("gpsimd", WG[:, :, :],
                   w["L1f_sg_t"][:, c0:c0 + 512].rearrange("(b p) n -> p b n", p=128))
            P.qdma("gpsimd", WU[:, :, :],
                   w["L1f_su_t"][:, c0:c0 + 512].rearrange("(b p) n -> p b n", p=128))
            P.qdma("gpsimd", WD[:, :, :],
                   w["L1f_sd_t"][c0:c0 + 512, :].rearrange("(b p) n -> p b n", p=128))
            for i in range(NT):
                P.qmm(PA[:, 0:512], [(XC[:, k, i * 128:(i + 1) * 128], WG[:, k, :])
                                     for k in range(ND)])
                P.q("scalar", lambda sc: sc.activation(HG[:], PA[:, 0:512], AF.Silu))
                P.qmm(PB2[:, 0:512], [(XC[:, k, i * 128:(i + 1) * 128], WU[:, k, :])
                                      for k in range(ND)])
                P.q("vector", lambda v: v.tensor_tensor(HID[:], PB2[:, 0:512], HG[:], ALU.mult))
                pbt = C.psum("PBtr", [128, 1024], BF16)

                def htr(t, pbt=pbt):
                    inst = None
                    for j in range(4):
                        inst = t.transpose(pbt[:, j * 128:(j + 1) * 128],
                                           HID[:, j * 128:(j + 1) * 128], ident[:])
                    return inst
                P.q("tensor", htr)
                P.q("vector", lambda v, pbt=pbt: v.tensor_copy(HIDC[:, :, :], pbt[:, 0:512]))
                for n in range(2):
                    P.qmm(PA[:, 0:512], [(HIDC[:, kb, :], WD[:, kb, n * 512:(n + 1) * 512])
                                         for kb in range(4)])
                    sl = slice(n * 512, (n + 1) * 512)
                    P.q("vector", lambda v, i=i, sl=sl: v.tensor_tensor(
                        XT[:, i, sl], XT[:, i, sl], PA[:, 0:512], ALU.add))
        for i in range(NT):
            P.q("vector", lambda v, i=i: v.tensor_tensor(XT[:, i, :], XT[:, i, :],
                                                         ACC[:, i, :], ALU.add))
            postln_tile(P, C, XT[:, i, :], STAT, SCR[:, :], eps=1e-5)
        P.qdma("sync", x_out[s].rearrange("(i p) d -> p i d", p=128), XT[:, :, :])
    P.finish()
    return names


def build_L2a(nc, es):
    P = Prog(nc, es)
    C = Ctx(P)
    x_in = nc.dram_tensor("x_in", [S_PER_CORE, T, D], F32, kind="ExternalInput").ap()
    x_out = nc.dram_tensor("x_out", [S_PER_CORE, T, D], F32, kind="ExternalOutput").ap()
    names = ["x_in", "c_scalars", "c_ident", "L2_wqkv_pg", "L2_wout"]
    shp = {"c_scalars": [128, len(CONST_VALS)], "c_ident": [128, 128],
           "L2_wqkv_pg": [8, D, 384], "L2_wout": [D, D]}
    w = {k: nc.dram_tensor(k, shp[k], F32, kind="ExternalInput").ap() for k in names[1:]}

    CONSTS = C.sbuf("CONSTS", [128, len(CONST_VALS)], F32)
    ident = C.sbuf("ident", [128, 128], BF16)
    XT = C.sbuf("XT", [128, NT, D], F32)
    XB = C.sbuf("XB", [128, D], BF16)
    XC = C.sbuf("XC", [128, ND, T], BF16)
    WP = C.sbuf("WP", [128, ND, 384], BF16)
    WO = C.sbuf("WO", [128, ND, D], BF16)
    QPC = C.sbuf("QPC", [128, T], BF16)
    KPC = C.sbuf("KPC", [128, T], BF16)
    VPT = C.sbuf("VPT", [128, NT, 128], BF16)
    SS = C.sbuf("SS", [128, T], F32)
    PP = C.sbuf("PP", [128, T], BF16)
    PT = C.sbuf("PT", [128, T], BF16)
    CTX = C.sbuf("CTX", [128, ND, T], BF16)
    STAT = C.sbuf("STAT", [128, 64], F32)
    SCR = C.sbuf("SCR", [128, D], F32)
    PA = C.psum("PA", [128, 512], F32)
    PB2 = C.psum("PB2", [128, 512], F32)
    PC = C.psum("PC", [128, 128], F32)

    P.qdma("gpsimd", CONSTS[:], w["c_scalars"])
    P.qdma("gpsimd", ident[:], w["c_ident"])
    P.qdma("gpsimd", WO[:, :, :], w["L2_wout"].rearrange("(b p) n -> p b n", p=128))

    for s in range(S_PER_CORE):
        P.qdma("sync", XT[:, :, :], x_in[s].rearrange("(i p) d -> p i d", p=128))
        for i in range(NT):
            P.q("vector", lambda v, i=i: v.tensor_copy(XB[:], XT[:, i, :]))
            tok_to_ch(P, C, XB[:], XC[:, :, i * 128:(i + 1) * 128])

        for p in range(8):
            P.qdma("gpsimd", WP[:, :, :], w["L2_wqkv_pg"][p].rearrange("(b p) n -> p b n", p=128))
            for dst, col0 in ((QPC, 0), (KPC, 128)):
                for c in range(4):
                    P.qmm(PA[:, 0:512],
                          [(WP[:, k, col0:col0 + 128], XC[:, k, c * 512:(c + 1) * 512])
                           for k in range(ND)])
                    P.q("vector", lambda v, dst=dst, c=c: v.tensor_copy(
                        dst[:, c * 512:(c + 1) * 512], PA[:, 0:512]))
            for i in range(NT):
                P.qmm(PC[:, 0:128],
                      [(XC[:, k, i * 128:(i + 1) * 128], WP[:, k, 256:384])
                       for k in range(ND)])
                P.q("scalar", lambda sc, i=i: sc.copy(VPT[:, i, :], PC[:, 0:128]))

            for hh in range(2):
                qsl = slice(64 * hh, 64 * hh + 64)
                for i in range(NT):
                    for c in range(4):
                        k0, k1 = c * 512, (c + 1) * 512
                        pa = PA if c % 2 == 0 else PB2
                        P.qmm(pa[:, 0:512],
                              [(QPC[qsl, i * 128:(i + 1) * 128], KPC[qsl, k0:k1])])
                        P.q("vector", lambda v, k0=k0, k1=k1, pa=pa: v.tensor_copy(
                            SS[:, k0:k1], pa[:, 0:512]))
                    P.q("vector", lambda v: v.reduce_max(
                        STAT[:, 8:9], SS[:, 0:T], axis=AX.X, negate=True))
                    P.q("vector", lambda v: v.tensor_scalar(
                        STAT[:, 9:10], STAT[:, 8:9], 1.0 / 32.0, None, ALU.mult))
                    P.q("scalar", lambda sc: sc.activation(
                        PP[:, 0:T], SS[:, 0:T], AF.Exp, bias=STAT[:, 9:10], scale=1.0 / 32.0,
                        accum_out=STAT[:, 10:11]))
                    P.q("vector", lambda v: v.reciprocal(STAT[:, 11:12], STAT[:, 10:11]))
                    P.q("vector", lambda v: v.tensor_scalar(
                        PP[:, 0:T], PP[:, 0:T], STAT[:, 11:12], None, ALU.mult))
                    for r0 in (0, 8):
                        pbt = C.psum("PBtr", [128, 1024], BF16)

                        def ptr(t, r0=r0, pbt=pbt):
                            inst = None
                            for j in range(r0, r0 + 8):
                                inst = t.transpose(pbt[:, (j - r0) * 128:(j - r0 + 1) * 128],
                                                   PP[:, j * 128:(j + 1) * 128], ident[:])
                            return inst
                        P.q("tensor", ptr)
                        P.q("vector", lambda v, r0=r0, pbt=pbt: v.tensor_copy(
                            PT[:, r0 * 128:(r0 + 8) * 128], pbt[:, 0:1024]))
                    P.qmm(PC[64 * hh:64 * hh + 64, 0:128],
                          [(VPT[:, j, qsl], PT[:, j * 128:(j + 1) * 128]) for j in range(NT)],
                          tile_position=(0, 64 * hh))
                    P.q("scalar", lambda sc, p=p, i=i, hh=hh: sc.copy(
                        CTX[64 * hh:64 * hh + 64, p, i * 128:(i + 1) * 128],
                        PC[64 * hh:64 * hh + 64, 0:128]))

        for i in range(NT):
            for n in range(2):
                P.qmm(PA[:, 0:512],
                      [(CTX[:, k, i * 128:(i + 1) * 128], WO[:, k, n * 512:(n + 1) * 512])
                       for k in range(ND)])
                sl = slice(n * 512, (n + 1) * 512)
                P.q("vector", lambda v, i=i, sl=sl: v.tensor_tensor(
                    XT[:, i, sl], XT[:, i, sl], PA[:, 0:512], ALU.add))
            postln_tile(P, C, XT[:, i, :], STAT, SCR[:, :], eps=1e-5)
        P.qdma("sync", x_out[s].rearrange("(i p) d -> p i d", p=128), XT[:, :, :])
    P.finish()
    return names


def build_L2f(nc, es):
    P = Prog(nc, es)
    C = Ctx(P)
    PD = 16
    W2C_W = PD + T + PD
    x_in = nc.dram_tensor("x_in", [S_PER_CORE, T, D], F32, kind="ExternalInput").ap()
    x_out = nc.dram_tensor("x_out", [S_PER_CORE, T, D], F32, kind="ExternalOutput").ap()
    names = ["x_in", "c_scalars", "c_ident", "c_l2f_mask", "L2f_wqkv_pg", "L2f_wo",
             "L2f_w1", "L2f_w2", "L2f_b1", "L2f_b2", "L2f_lng", "L2f_lnb",
             "L2f_qs", "L2f_ks", "L2f_cq", "L2f_sq", "L2f_ck", "L2f_sk"]
    shp = {"c_scalars": [128, len(CONST_VALS)], "c_ident": [128, 128],
           "c_l2f_mask": [4, 128, 1024], "L2f_wqkv_pg": [2, 4, D, 384],
           "L2f_wo": [2, 512, D], "L2f_w1": [2, D, 5460], "L2f_w2": [2, 2730, D],
           "L2f_b1": [2, 128, 5460], "L2f_b2": [2, 128, D],
           "L2f_lng": [2, 128, D], "L2f_lnb": [2, 128, D],
           "L2f_qs": [2, 128, 128], "L2f_ks": [2, 128, 128],
           "L2f_cq": [NT, 128, 128], "L2f_sq": [NT, 128, 128],
           "L2f_ck": [NT, 128, 128], "L2f_sk": [NT, 128, 128]}
    w = {k: nc.dram_tensor(k, shp[k], F32, kind="ExternalInput").ap() for k in names[1:]}

    CONSTS = C.sbuf("CONSTS", [128, len(CONST_VALS)], F32)
    ident = C.sbuf("ident", [128, 128], BF16)
    X2T = C.sbuf("X2T", [128, NT, D], F32)
    XB = C.sbuf("XB", [128, D], BF16)
    STAT = C.sbuf("STAT", [128, 64], F32)
    SCR = C.sbuf("SCR", [128, D], F32)
    ARE = C.sbuf("ARE", [128, 99 * 512], BF16)

    def V(off_kb, shape, dtype=BF16):
        esz = 2 if dtype == BF16 else 4
        n = int(np.prod(shape[1:]))
        ap = ARE.ap()[:, off_kb * 512: off_kb * 512 + (n * esz) // 2]
        if dtype == F32:
            ap = ap.bitcast(F32)
        if len(shape) == 3:
            ap = ap.rearrange("p (a b) -> p a b", a=shape[1])
        return ap

    X2C = V(0, [128, ND, W2C_W])
    WQKVP = V(33, [128, ND, 384])
    QTP = V(39, [128, NT, 128])
    KTP = V(43, [128, NT, 128])
    VTP = V(47, [128, NT, 128])
    QPC = V(51, [128, T])
    KPC = V(55, [128, T])
    RCQ = V(59, [128, 128], F32)
    RSQ = V(60, [128, 128], F32)
    RCK = V(61, [128, 128], F32)
    RSK = V(62, [128, 128], F32)
    QST = V(63, [128, 128], F32)
    KST = V(64, [128, 128], F32)
    MASKB = V(65, [128, 1024], F32)
    SS = V(69, [128, 1024], F32)
    PP = V(73, [128, 1024])
    PT = V(75, [128, 1024])
    CTX2 = V(77, [128, 4, T])
    RA = V(93, [128, 128], F32)
    RB = V(94, [128, 128], F32)
    WO2 = V(33, [128, 4, D])
    HT = V(0, [128, NT, D])
    NC2 = V(33, [128, ND, T])
    W1A = V(66, [128, ND, 512])
    W1G = V(74, [128, ND, 512])
    W2CH = V(82, [128, 4, D])
    AB = V(90, [128, 512], F32)
    GB = V(92, [128, 512])
    TB2 = V(93, [128, 512])
    TC = V(94, [128, 4, 128])
    B1A = V(95, [128, 512], F32)
    B1G = V(97, [128, 512], F32)
    B2T = V(69, [128, D], F32)
    LG = V(73, [128, D], F32)
    LB = V(77, [128, D], F32)

    PA = C.psum("PA", [128, 512], F32)
    PB2 = C.psum("PB2", [128, 512], F32)
    PC = C.psum("PC", [128, 128], F32)

    P.qdma("gpsimd", CONSTS[:], w["c_scalars"])
    P.qdma("gpsimd", ident[:], w["c_ident"])

    HALF = np.float32(0.5)

    def rothalf(src_ap, dst_ap):
        sh = bass.AP(src_ap.tensor, src_ap.offset, src_ap.ap)
        d0 = dst_ap.rearrange("p (h c) -> p h c", h=2)
        s0 = src_ap.rearrange("p (h c) -> p h c", h=2)
        P.q("vector", lambda v: v.tensor_scalar(d0[:, :, 0:32], s0[:, :, 32:64],
                                                -1.0, None, ALU.mult))
        P.q("vector", lambda v: v.tensor_copy(d0[:, :, 32:64], s0[:, :, 0:32]))

    for s in range(S_PER_CORE):
        P.qdma("sync", X2T[:, :, :], x_in[s].rearrange("(i p) d -> p i d", p=128))
        P.q("gpsimd", lambda g: g.memset(X2C[:, :, 0:PD], 0.0))
        P.q("gpsimd", lambda g: g.memset(X2C[:, :, PD + T:], 0.0))

        for l in range(2):
            for i in range(NT):
                P.q("vector", lambda v, i=i: v.tensor_copy(XB[:], X2T[:, i, :]))
                tok_to_ch(P, C, XB[:], X2C[:, :, PD + i * 128:PD + (i + 1) * 128])
            P.qdma("gpsimd", QST, w["L2f_qs"][l])
            P.qdma("gpsimd", KST, w["L2f_ks"][l])

            def shlhs(k, i):
                off = PD + i * 128 - (1 if k >= 4 else 0)
                return X2C[:, k, off:off + 128]

            for hp in range(4):
                P.qdma("gpsimd", WQKVP, w["L2f_wqkv_pg"][l, hp].rearrange("(b p) n -> p b n", p=128))
                for i in range(NT):
                    P.qdma("gpsimd", RCQ, w["L2f_cq"][i])
                    P.qdma("gpsimd", RSQ, w["L2f_sq"][i])
                    P.qdma("gpsimd", RCK, w["L2f_ck"][i])
                    P.qdma("gpsimd", RSK, w["L2f_sk"][i])
                    for which, dstT, col0 in (("q", QTP, 0), ("k", KTP, 128), ("v", VTP, 256)):
                        P.qmm(PC[:, 0:128], [(shlhs(k, i), WQKVP[:, k, col0:col0 + 128])
                                             for k in range(ND)])
                        if which == "v":
                            P.q("scalar", lambda sc, i=i: sc.copy(VTP[:, i, :], PC[:, 0:128]))
                            continue
                        for hh2 in range(2):
                            hs = slice(64 * hh2, 64 * hh2 + 64)
                            P.q("scalar", lambda sc, hs=hs: sc.activation(
                                SCR[:, 0:64], PC[:, hs], AF.Square, accum_out=STAT[:, 4:5]))
                            P.q("scalar", lambda sc: sc.activation(
                                STAT[:, 5:6], STAT[:, 4:5], AF.Sqrt, bias=C.cs(0.0), scale=1.0))
                            P.q("vector", lambda v: v.tensor_scalar(
                                STAT[:, 5:6], STAT[:, 5:6], 1e-12, None, ALU.max))
                            P.q("vector", lambda v: v.reciprocal(STAT[:, 6:7], STAT[:, 5:6]))
                            P.q("vector", lambda v, hs=hs: v.tensor_scalar(
                                RA[:, hs], PC[:, hs], STAT[:, 6:7], None, ALU.mult))
                        sct = QST if which == "q" else KST
                        P.q("vector", lambda v, sct=sct: v.tensor_tensor(RA, RA, sct, ALU.mult))
                        rothalf(RA, RB)
                        ctab = RCQ if which == "q" else RCK
                        stab = RSQ if which == "q" else RSK
                        P.q("vector", lambda v, ctab=ctab: v.tensor_tensor(RA, RA, ctab, ALU.mult))
                        P.q("vector", lambda v, stab=stab: v.tensor_tensor(RB, RB, stab, ALU.mult))
                        P.q("vector", lambda v, i=i, dstT=dstT: v.tensor_tensor(
                            dstT[:, i, :], RA, RB, ALU.add))
                for i in range(NT):
                    pbt = C.psum("PBtr", [128, 1024], BF16)
                    P.q("tensor", lambda t, i=i, pbt=pbt: t.transpose(pbt[:, 0:128], QTP[:, i, :], ident[:]))
                    P.q("vector", lambda v, i=i, pbt=pbt: v.tensor_copy(
                        QPC[:, i * 128:(i + 1) * 128], pbt[:, 0:128]))
                    P.q("tensor", lambda t, i=i, pbt=pbt: t.transpose(pbt[:, 128:256], KTP[:, i, :], ident[:]))
                    P.q("vector", lambda v, i=i, pbt=pbt: v.tensor_copy(
                        KPC[:, i * 128:(i + 1) * 128], pbt[:, 128:256]))

                for hh in range(2):
                    qsl = slice(64 * hh, 64 * hh + 64)
                    for win in range(4):
                        for sdx in range(4):
                            i = 4 * win + sdx
                            P.qdma("gpsimd", MASKB, w["c_l2f_mask"][sdx])
                            if win == 0:
                                nk, ktile0 = 512, 0
                                P.qmm(PA[:, 0:512],
                                      [(QPC[qsl, i * 128:(i + 1) * 128], KPC[qsl, 0:512])])
                                P.q("vector", lambda v: v.tensor_tensor(
                                    SS[:, 0:512], PA[:, 0:512], MASKB[:, 512:1024], ALU.add))
                            else:
                                nk, ktile0 = 1024, 4 * (win - 1)
                                k0t = 512 * (win - 1)
                                for c in range(2):
                                    pa = PA if c == 0 else PB2
                                    P.qmm(pa[:, 0:512],
                                          [(QPC[qsl, i * 128:(i + 1) * 128],
                                            KPC[qsl, k0t + c * 512:k0t + (c + 1) * 512])])
                                    P.q("vector", lambda v, c=c, pa=pa: v.tensor_tensor(
                                        SS[:, c * 512:(c + 1) * 512], pa[:, 0:512],
                                        MASKB[:, c * 512:(c + 1) * 512], ALU.add))
                            P.q("vector", lambda v, nk=nk: v.reduce_max(
                                STAT[:, 8:9], SS[:, 0:nk], axis=AX.X, negate=True))
                            P.q("scalar", lambda sc, nk=nk: sc.activation(
                                PP[:, 0:nk], SS[:, 0:nk], AF.Exp, bias=STAT[:, 8:9],
                                scale=1.0, accum_out=STAT[:, 10:11]))
                            P.q("vector", lambda v: v.reciprocal(STAT[:, 11:12], STAT[:, 10:11]))
                            P.q("vector", lambda v, nk=nk: v.tensor_scalar(
                                PP[:, 0:nk], PP[:, 0:nk], STAT[:, 11:12], None, ALU.mult))
                            nkt = nk // 128
                            pbt = C.psum("PBtr", [128, 1024], BF16)

                            def ptr(t, nkt=nkt, pbt=pbt):
                                inst = None
                                for j in range(nkt):
                                    inst = t.transpose(pbt[:, j * 128:(j + 1) * 128],
                                                       PP[:, j * 128:(j + 1) * 128], ident[:])
                                return inst
                            P.q("tensor", ptr)
                            P.q("vector", lambda v, nkt=nkt, pbt=pbt: v.tensor_copy(
                                PT[:, 0:nkt * 128], pbt[:, 0:nkt * 128]))
                            P.qmm(PC[64 * hh:64 * hh + 64, 0:128],
                                  [(VTP[:, ktile0 + j, qsl], PT[:, j * 128:(j + 1) * 128])
                                   for j in range(nkt)],
                                  tile_position=(0, 64 * hh))
                            P.q("scalar", lambda sc, hp=hp, i=i, hh=hh: sc.copy(
                                CTX2[64 * hh:64 * hh + 64, hp, i * 128:(i + 1) * 128],
                                PC[64 * hh:64 * hh + 64, 0:128]))

            P.qdma("gpsimd", WO2, w["L2f_wo"][l].rearrange("(b p) n -> p b n", p=128))
            for i in range(NT):
                for n in range(2):
                    P.qmm(PA[:, 0:512],
                          [(CTX2[:, kb, i * 128:(i + 1) * 128], WO2[:, kb, n * 512:(n + 1) * 512])
                           for kb in range(4)])
                    sl = slice(n * 512, (n + 1) * 512)
                    P.q("vector", lambda v, i=i, sl=sl: v.tensor_tensor(
                        X2T[:, i, sl], X2T[:, i, sl], PA[:, 0:512], ALU.add))

            P.qdma("gpsimd", LG, w["L2f_lng"][l])
            P.qdma("gpsimd", LB, w["L2f_lnb"][l])
            for i in range(NT):
                P.q("vector", lambda v, i=i: v.tensor_copy(HT[:, i, 0:512], X2T[:, i, 0:512]))
                P.qdma("gpsimd", HT[1:128, i, 512:1024], X2T[0:127, i, 512:1024])
                if i == 0:
                    P.q("gpsimd", lambda g: g.memset(HT[0:1, 0, 512:1024], 0.0))
                else:
                    P.qdma("gpsimd", HT[0:1, i, 512:1024], X2T[127:128, i - 1, 512:1024])
            for i in range(NT):
                postln_tile(P, C, HT[:, i, :], STAT, SCR[:, :], eps=1e-5,
                            gamma=LG, beta=LB, out_ap=XB[:])
                tok_to_ch(P, C, XB[:], NC2[:, :, i * 128:(i + 1) * 128])

            P.qdma("gpsimd", B2T, w["L2f_b2"][l])
            chunks = [(c, 512) for c in range(0, 2560, 512)] + [(2560, 170)]
            for (c0, cs) in chunks:
                P.qdma("gpsimd", W1A[:, :, 0:cs].opt() if cs != 512 else W1A,
                       w["L2f_w1"][l][:, c0:c0 + cs].rearrange("(b p) n -> p b n", p=128))
                P.qdma("gpsimd", W1G[:, :, 0:cs].opt() if cs != 512 else W1G,
                       w["L2f_w1"][l][:, 2730 + c0:2730 + c0 + cs].rearrange("(b p) n -> p b n", p=128))
                subs = [(r, min(128, cs - r)) for r in range(0, cs, 128)]
                for (r0, rs) in subs:
                    P.qdma("gpsimd", W2CH[0:rs, r0 // 128, :],
                           w["L2f_w2"][l][c0 + r0:c0 + r0 + rs, :])
                P.qdma("gpsimd", B1A[:, 0:cs], w["L2f_b1"][l][:, c0:c0 + cs])
                P.qdma("gpsimd", B1G[:, 0:cs], w["L2f_b1"][l][:, 2730 + c0:2730 + c0 + cs])
                for i in range(NT):
                    P.qmm(PA[:, 0:cs], [(NC2[:, k, i * 128:(i + 1) * 128], W1A[:, k, 0:cs])
                                        for k in range(ND)])
                    P.q("vector", lambda v, cs=cs: v.tensor_tensor(
                        AB[:, 0:cs], PA[:, 0:cs], B1A[:, 0:cs], ALU.add))
                    P.qmm(PB2[:, 0:cs], [(NC2[:, k, i * 128:(i + 1) * 128], W1G[:, k, 0:cs])
                                         for k in range(ND)])
                    P.q("vector", lambda v, cs=cs: v.tensor_tensor(
                        SCR[:, 0:cs], PB2[:, 0:cs], B1G[:, 0:cs], ALU.add))
                    P.q("scalar", lambda sc, cs=cs: sc.activation(GB[:, 0:cs], SCR[:, 0:cs], AF.Gelu))
                    P.q("vector", lambda v, cs=cs: v.tensor_tensor(
                        TB2[:, 0:cs], AB[:, 0:cs], GB[:, 0:cs], ALU.mult))
                    pbt = C.psum("PBtr", [128, 1024], BF16)

                    def htr(t, subs=subs, pbt=pbt):
                        inst = None
                        for si, (r0, rs) in enumerate(subs):
                            inst = t.transpose(pbt[0:rs, si * 128:(si + 1) * 128],
                                               TB2[:, r0:r0 + rs], ident[:])
                        return inst
                    P.q("tensor", htr)
                    for si, (r0, rs) in enumerate(subs):
                        P.q("vector", lambda v, si=si, rs=rs, pbt=pbt: v.tensor_copy(
                            TC[0:rs, si, :], pbt[0:rs, si * 128:(si + 1) * 128]))
                    for n in range(2):
                        P.qmm(PA[:, 0:512],
                              [(TC[0:rs, si, :], W2CH[0:rs, si, n * 512:(n + 1) * 512])
                               for si, (r0, rs) in enumerate(subs)])
                        sl = slice(n * 512, (n + 1) * 512)
                        P.q("vector", lambda v, i=i, sl=sl: v.tensor_tensor(
                            X2T[:, i, sl], X2T[:, i, sl], PA[:, 0:512], ALU.add))
            for i in range(NT):
                P.q("vector", lambda v, i=i: v.tensor_tensor(
                    X2T[:, i, :], X2T[:, i, :], B2T, ALU.add))

        for i in range(NT):
            P.qdma("sync", SCR[:, :], x_in[s][i * 128:(i + 1) * 128, :])
            P.q("vector", lambda v, i=i: v.tensor_tensor(
                X2T[:, i, :], X2T[:, i, :], SCR[:, :], ALU.add))
            postln_tile(P, C, X2T[:, i, :], STAT, SCR[:, :], eps=1e-5)
        P.qdma("sync", x_out[s].rearrange("(i p) d -> p i d", p=128), X2T[:, :, :])
    P.finish()
    return names


def build_L3a(nc, es):
    P = Prog(nc, es)
    C = Ctx(P)
    PD = 16
    x_in = nc.dram_tensor("x_in", [S_PER_CORE, T, D], F32, kind="ExternalInput").ap()
    x_out = nc.dram_tensor("x_out", [S_PER_CORE, T, D], F32, kind="ExternalOutput").ap()
    names = ["x_in", "c_scalars", "c_ident", "c_ones", "L3a_wqkv_pg", "L3a_wproj_t",
             "L3a_bproj", "L3a_wdwc", "L3a_bdwc"]
    shp = {"c_scalars": [128, len(CONST_VALS)], "c_ident": [128, 128],
           "c_ones": [128, 128], "L3a_wqkv_pg": [8, D, 384], "L3a_wproj_t": [D, D],
           "L3a_bproj": [128, D], "L3a_wdwc": [128, 8, 3], "L3a_bdwc": [128, 8]}
    w = {k: nc.dram_tensor(k, shp[k], F32, kind="ExternalInput").ap() for k in names[1:]}

    CONSTS = C.sbuf("CONSTS", [128, len(CONST_VALS)], F32)
    ident = C.sbuf("ident", [128, 128], BF16)
    ones = C.sbuf("ones", [128, 128], BF16)
    wdwc = C.sbuf("wdwc", [128, 8, 3], F32)
    bdwc = C.sbuf("bdwc", [128, 8], F32)
    bproj = C.sbuf("bproj", [128, D], F32)
    XT = C.sbuf("XT", [128, NT, D], F32)
    XB = C.sbuf("XB", [128, D], BF16)
    STAT = C.sbuf("STAT", [128, 64], F32)
    SCR = C.sbuf("SCR", [128, D], F32)
    ARE = C.sbuf("ARE", [128, 100 * 512], BF16)

    def V(off_kb, shape, dtype=BF16):
        esz = 2 if dtype == BF16 else 4
        n = int(np.prod(shape[1:]))
        ap = ARE.ap()[:, off_kb * 512: off_kb * 512 + (n * esz) // 2]
        if dtype == F32:
            ap = ap.bitcast(F32)
        if len(shape) == 3:
            ap = ap.rearrange("p (a b) -> p a b", a=shape[1])
        return ap

    XC = V(0, [128, ND, T])
    OUTC = V(33, [128, ND, PD + T + PD])
    WP = V(67, [128, ND, 384])
    QF = V(73, [128, NT, 128], F32)
    KN = V(81, [128, NT, 128])
    VTP = V(85, [128, NT, 128])
    QZT = V(89, [128, NT, 128])
    QZC = V(93, [128, T])
    KV = V(97, [128, 64])
    KSB = V(98, [128, 128], F32)
    OUTSUM = V(0, [128, ND, T])
    WPJ = V(67, [128, ND, D])
    CS = V(83, [128, T], F32)
    PA = C.psum("PA", [128, 512], F32)
    PC = C.psum("PC", [128, 128], F32)
    PD2 = C.psum("PD2", [128, 128], F32)

    P.qdma("gpsimd", CONSTS[:], w["c_scalars"])
    P.qdma("gpsimd", ident[:], w["c_ident"])
    P.qdma("gpsimd", ones[:], w["c_ones"])
    P.qdma("gpsimd", wdwc[:, :, :], w["L3a_wdwc"])
    P.qdma("gpsimd", bdwc[:, :], w["L3a_bdwc"])
    P.qdma("gpsimd", bproj[:], w["L3a_bproj"])

    for s in range(S_PER_CORE):
        P.qdma("sync", XT[:, :, :], x_in[s].rearrange("(i p) d -> p i d", p=128))
        P.q("gpsimd", lambda g: g.memset(OUTC[:, :, 0:PD], 0.0))
        P.q("gpsimd", lambda g: g.memset(OUTC[:, :, PD + T:], 0.0))
        for i in range(NT):
            P.q("vector", lambda v, i=i: v.tensor_copy(XB[:], XT[:, i, :]))
            tok_to_ch(P, C, XB[:], XC[:, :, i * 128:(i + 1) * 128])

        for p in range(8):
            P.qdma("gpsimd", WP, w["L3a_wqkv_pg"][p].rearrange("(b p) n -> p b n", p=128))
            for which, col0 in (("q", 0), ("k", 128), ("v", 256)):
                for i in range(NT):
                    P.qmm(PC[:, 0:128],
                          [(XC[:, k, i * 128:(i + 1) * 128], WP[:, k, col0:col0 + 128])
                           for k in range(ND)])
                    if which == "v":
                        P.q("scalar", lambda sc, i=i: sc.copy(VTP[:, i, :], PC[:, 0:128]))
                        continue
                    for hh in range(2):
                        hs = slice(64 * hh, 64 * hh + 64)
                        P.q("scalar", lambda sc, hs=hs: sc.activation(
                            SCR[:, 0:64], PC[:, hs], AF.Relu))
                        P.q("vector", lambda v: v.tensor_scalar(
                            SCR[:, 0:64], SCR[:, 0:64], C.cs(1e-6), None, ALU.add))
                        P.q("vector", lambda v: v.reduce_sum(STAT[:, 0:1], SCR[:, 0:64], axis=AX.X))
                        P.q("vector", lambda v: v.reciprocal(STAT[:, 1:2], STAT[:, 0:1]))
                        P.q("vector", lambda v: v.tensor_scalar(
                            SCR[:, 0:64], SCR[:, 0:64], STAT[:, 1:2], None, ALU.mult))
                        P.q("scalar", lambda sc: sc.activation(
                            SCR[:, 64:128], SCR[:, 0:64], AF.Square))
                        P.q("vector", lambda v: v.tensor_tensor(
                            SCR[:, 64:128], SCR[:, 64:128], SCR[:, 0:64], ALU.mult))
                        P.q("vector", lambda v: v.reduce_sum(STAT[:, 2:3], SCR[:, 64:128], axis=AX.X))
                        P.q("vector", lambda v: v.reciprocal(STAT[:, 3:4], STAT[:, 2:3]))
                        if which == "q":
                            P.q("vector", lambda v, i=i, hs=hs: v.tensor_scalar(
                                QF[:, i, hs], SCR[:, 64:128], STAT[:, 3:4], None, ALU.mult))
                        else:
                            P.q("vector", lambda v, i=i, hs=hs: v.tensor_scalar(
                                KN[:, i, hs], SCR[:, 64:128], STAT[:, 3:4], None, ALU.mult))
            for hh in range(2):
                hs = slice(64 * hh, 64 * hh + 64)
                P.qmm(PD2[64 * hh:64 * hh + 64, 0:64],
                      [(KN[:, j, hs], VTP[:, j, hs]) for j in range(NT)],
                      tile_position=(0, 64 * hh))
            P.q("vector", lambda v: v.tensor_copy(KV[:, 0:64], PD2[:, 0:64]))
            P.qmm(PC[0:1, 0:128], [(ones[:, 0:1], KN[:, j, :]) for j in range(NT)])
            P.q("vector", lambda v: v.tensor_copy(KSB[0:1, :], PC[0:1, 0:128]))
            P.q("vector", lambda v: v.tensor_copy(XB[0:1, 0:128], KSB[0:1, :]))
            P.qmm(PC[:, 0:128], [(ones[0:1, :], XB[0:1, 0:128])])
            P.q("vector", lambda v: v.tensor_copy(KSB[:, :], PC[:, 0:128]))
            for i in range(NT):
                P.q("vector", lambda v, i=i: v.tensor_tensor(
                    SCR[:, 0:128], QF[:, i, :], KSB[:, :], ALU.mult))
                for hh in range(2):
                    hs = slice(64 * hh, 64 * hh + 64)
                    P.q("vector", lambda v, hs=hs: v.reduce_sum(
                        STAT[:, 4:5], SCR[:, hs], axis=AX.X))
                    P.q("vector", lambda v: v.tensor_scalar(
                        STAT[:, 4:5], STAT[:, 4:5], C.cs(1e-6), None, ALU.add))
                    P.q("vector", lambda v: v.reciprocal(STAT[:, 5:6], STAT[:, 4:5]))
                    P.q("vector", lambda v, i=i, hs=hs: v.tensor_scalar(
                        QZT[:, i, hs], QF[:, i, hs], STAT[:, 5:6], None, ALU.mult))
            for i in range(NT):
                pbt = C.psum("PBtr", [128, 1024], BF16)
                P.q("tensor", lambda t, i=i, pbt=pbt: t.transpose(pbt[:, 0:128], QZT[:, i, :], ident[:]))
                P.q("vector", lambda v, i=i, pbt=pbt: v.tensor_copy(
                    QZC[:, i * 128:(i + 1) * 128], pbt[:, 0:128]))
            for hh in range(2):
                for c in range(4):
                    P.qmm(PA[64 * hh:64 * hh + 64, 0:512],
                          [(KV[64 * hh:64 * hh + 64, 0:64],
                            QZC[64 * hh:64 * hh + 64, c * 512:(c + 1) * 512])],
                          tile_position=(64 * hh, 64 * hh))
                    P.q("scalar", lambda sc, p=p, hh=hh, c=c: sc.copy(
                        OUTC[64 * hh:64 * hh + 64, p, PD + c * 512:PD + (c + 1) * 512],
                        PA[64 * hh:64 * hh + 64, 0:512]))

        for b in range(ND):
            P.q("vector", lambda v, b=b: v.tensor_scalar(
                CS[:, :], OUTC[:, b, PD - 1:PD - 1 + T], wdwc[:, b, 0:1], None, ALU.mult))
            P.q("vector", lambda v, b=b: v.scalar_tensor_tensor(
                CS[:, :], OUTC[:, b, PD:PD + T], wdwc[:, b, 1:2], CS[:, :], ALU.mult, ALU.add))
            P.q("vector", lambda v, b=b: v.scalar_tensor_tensor(
                CS[:, :], OUTC[:, b, PD + 1:PD + 1 + T], wdwc[:, b, 2:3], CS[:, :], ALU.mult, ALU.add))
            P.q("vector", lambda v, b=b: v.tensor_scalar(
                CS[:, :], CS[:, :], bdwc[:, b:b + 1], None, ALU.add))
            P.q("vector", lambda v, b=b: v.tensor_tensor(
                OUTSUM[:, b, :], CS[:, :], OUTC[:, b, PD:PD + T], ALU.add))

        P.qdma("gpsimd", WPJ, w["L3a_wproj_t"].rearrange("(b p) n -> p b n", p=128))
        for i in range(NT):
            for n in range(2):
                P.qmm(PA[:, 0:512],
                      [(OUTSUM[:, k, i * 128:(i + 1) * 128], WPJ[:, k, n * 512:(n + 1) * 512])
                       for k in range(ND)])
                sl = slice(n * 512, (n + 1) * 512)
                P.q("vector", lambda v, sl=sl: v.tensor_tensor(
                    SCR[:, sl], PA[:, 0:512], bproj[:, sl], ALU.add))
                P.q("vector", lambda v, i=i, sl=sl: v.tensor_tensor(
                    XT[:, i, sl], XT[:, i, sl], SCR[:, sl], ALU.add))
            postln_tile(P, C, XT[:, i, :], STAT, SCR[:, :], eps=1e-5)
        P.qdma("sync", x_out[s].rearrange("(i p) d -> p i d", p=128), XT[:, :, :])
    P.finish()
    return names


def build_L3f(nc, es):
    P = Prog(nc, es)
    C = Ctx(P)
    PD = 16
    x_in = nc.dram_tensor("x_in", [S_PER_CORE, T, D], F32, kind="ExternalInput").ap()
    x_out = nc.dram_tensor("x_out", [S_PER_CORE, T, D], F32, kind="ExternalOutput").ap()
    names = ["x_in", "c_scalars", "c_ident", "c_iota32", "c_idxm32",
             "L3f_wr", "L3f_rbias", "L3f_wg", "L3f_wu", "L3f_wd",
             "L3f_convw", "L3f_convb"]
    shp = {"c_scalars": [128, len(CONST_VALS)], "c_ident": [128, 128],
           "c_iota32": [128, 32], "c_idxm32": [128, 32],
           "L3f_wr": [D, 8], "L3f_rbias": [128, 8],
           "L3f_wg": [4, D, 512], "L3f_wu": [4, D, 512], "L3f_wd": [4, 512, D],
           "L3f_convw": [128, 8, 4], "L3f_convb": [128, 8]}
    w = {k: nc.dram_tensor(k, shp[k], F32, kind="ExternalInput").ap() for k in names[1:]}

    CONSTS = C.sbuf("CONSTS", [128, len(CONST_VALS)], F32)
    ident = C.sbuf("ident", [128, 128], BF16)
    iota32 = C.sbuf("iota32", [128, 32], F32)
    idxm32 = C.sbuf("idxm32", [128, 32], F32)
    convw = C.sbuf("convw", [128, 8, 4], F32)
    convb = C.sbuf("convb", [128, 8], F32)
    rbias = C.sbuf("rbias", [128, 8], F32)
    wrt = C.sbuf("wrt", [128, ND, 8], BF16)
    XT = C.sbuf("XT", [128, NT, D], F32)
    XB = C.sbuf("XB", [128, D], BF16)
    XC = C.sbuf("XCp", [128, ND, PD + T + PD], BF16)
    CONVC = C.sbuf("CONVC", [128, ND, T], BF16)
    CW = C.sbuf("CW", [128, NT, 8], F32)
    RT = C.sbuf("RT", [128, 64], F32)
    WG = C.sbuf("WG", [128, ND, 512], BF16)
    WU = C.sbuf("WU", [128, ND, 512], BF16)
    WD = C.sbuf("WD", [128, 4, D], BF16)
    HG = C.sbuf("HG", [128, 512], BF16)
    HID = C.sbuf("HID", [128, 512], BF16)
    HIDC = C.sbuf("HIDC", [128, 4, 128], BF16)
    STAT = C.sbuf("STAT", [128, 64], F32)
    SCR = C.sbuf("SCR", [128, D], F32)
    CS = C.sbuf("CS", [128, T], F32)
    PA = C.psum("PA", [128, 512], F32)
    PB2 = C.psum("PB2", [128, 512], F32)

    P.qdma("gpsimd", CONSTS[:], w["c_scalars"])
    P.qdma("gpsimd", ident[:], w["c_ident"])
    P.qdma("gpsimd", iota32[:], w["c_iota32"])
    P.qdma("gpsimd", idxm32[:], w["c_idxm32"])
    P.qdma("gpsimd", convw[:, :, :], w["L3f_convw"])
    P.qdma("gpsimd", convb[:, :], w["L3f_convb"])
    P.qdma("gpsimd", rbias[:], w["L3f_rbias"])
    P.qdma("gpsimd", wrt[:, :, :], w["L3f_wr"].rearrange("(b p) n -> p b n", p=128))

    for s in range(S_PER_CORE):
        P.qdma("sync", XT[:, :, :], x_in[s].rearrange("(i p) d -> p i d", p=128))
        P.q("gpsimd", lambda g: g.memset(XC[:, :, 0:PD], 0.0))
        P.q("gpsimd", lambda g: g.memset(XC[:, :, PD + T:], 0.0))
        for i in range(NT):
            P.q("vector", lambda v, i=i: v.tensor_copy(XB[:], XT[:, i, :]))
            tok_to_ch(P, C, XB[:], XC[:, :, PD + i * 128:PD + (i + 1) * 128])

        SC8 = RT[:, 0:8]; CH8 = RT[:, 8:16]; EQ1 = RT[:, 16:24]; EQ2 = RT[:, 24:32]
        TMP8 = RT[:, 32:40]; TB = RT[:, 40:48]; SEL = RT[:, 48:56]
        for i in range(NT):
            P.qmm(PA[:, 0:8], [(XC[:, k, PD + i * 128:PD + (i + 1) * 128], wrt[:, k, :])
                               for k in range(ND)])
            P.q("scalar", lambda sc: sc.activation(SC8, PA[:, 0:8], AF.Sigmoid))
            P.q("vector", lambda v: v.tensor_tensor(CH8, SC8, rbias[:], ALU.add))
            argmax_first(P, CH8, iota32[:, 0:8], idxm32[:, 0:8], EQ1, TMP8, TB)
            P.q("vector", lambda v: v.scalar_tensor_tensor(
                TMP8, EQ1, -1e30, CH8, ALU.mult, ALU.add))
            argmax_first(P, TMP8, iota32[:, 0:8], idxm32[:, 0:8], EQ2, RT[:, 56:64], TB)
            P.q("vector", lambda v: v.tensor_tensor(SEL, EQ1, EQ2, ALU.add))
            P.q("vector", lambda v: v.tensor_tensor(TMP8, SEL, SC8, ALU.mult))
            P.q("vector", lambda v: v.reduce_sum(TB[:, 2:3], TMP8, axis=AX.X))
            P.q("vector", lambda v: v.tensor_scalar(TB[:, 2:3], TB[:, 2:3], C.cs(1e-9), None, ALU.add))
            P.q("vector", lambda v: v.reciprocal(TB[:, 3:4], TB[:, 2:3]))
            P.q("vector", lambda v, i=i: v.tensor_scalar(CW[:, i, :], TMP8, TB[:, 3:4], None, ALU.mult))

        for i in range(NT):
            P.q("vector", lambda v, i=i: v.tensor_tensor(
                STAT[:, 8:9], CW[:, i, 4:5], CW[:, i, 6:7], ALU.add))
            P.q("scalar", lambda sc, i=i: sc.activation(SCR[:], XT[:, i, :], AF.Relu))
            P.q("vector", lambda v, i=i: v.scalar_tensor_tensor(
                SCR[:], SCR[:], CW[:, i, 7:8], XT[:, i, :], ALU.mult, ALU.bypass)
                if False else v.tensor_scalar(SCR[:], SCR[:], CW[:, i, 7:8], None, ALU.mult))
            P.q("vector", lambda v, i=i: v.scalar_tensor_tensor(
                SCR[:], XT[:, i, :], STAT[:, 8:9], SCR[:], ALU.mult, ALU.add))
            P.q("vector", lambda v, i=i: v.tensor_tensor(
                XT[:, i, :], XT[:, i, :], SCR[:], ALU.add))

        for b in range(ND):
            P.q("vector", lambda v, b=b: v.tensor_scalar(
                CS[:, :], XC[:, b, PD - 3:PD - 3 + T], convw[:, b, 0:1], None, ALU.mult))
            for kk in range(1, 4):
                P.q("vector", lambda v, b=b, kk=kk: v.scalar_tensor_tensor(
                    CS[:, :], XC[:, b, PD - 3 + kk:PD - 3 + kk + T], convw[:, b, kk:kk + 1],
                    CS[:, :], ALU.mult, ALU.add))
            P.q("vector", lambda v, b=b: v.tensor_scalar(
                CONVC[:, b, :], CS[:, :], convb[:, b:b + 1], None, ALU.add))

        for e in range(4):
            P.qdma("gpsimd", WG[:, :, :], w["L3f_wg"][e].rearrange("(b p) n -> p b n", p=128))
            P.qdma("gpsimd", WU[:, :, :], w["L3f_wu"][e].rearrange("(b p) n -> p b n", p=128))
            P.qdma("gpsimd", WD[:, :, :], w["L3f_wd"][e].rearrange("(b p) n -> p b n", p=128))
            for i in range(NT):
                P.qmm(PA[:, 0:512], [(XC[:, k, PD + i * 128:PD + (i + 1) * 128], WG[:, k, :])
                                     for k in range(ND)])
                P.q("scalar", lambda sc: sc.activation(HG[:], PA[:, 0:512], AF.Silu))
                P.qmm(PB2[:, 0:512], [(XC[:, k, PD + i * 128:PD + (i + 1) * 128], WU[:, k, :])
                                      for k in range(ND)])
                P.q("vector", lambda v: v.tensor_tensor(HID[:], PB2[:, 0:512], HG[:], ALU.mult))
                pbt = C.psum("PBtr", [128, 1024], BF16)

                def htr(t, pbt=pbt):
                    inst = None
                    for j in range(4):
                        inst = t.transpose(pbt[:, j * 128:(j + 1) * 128],
                                           HID[:, j * 128:(j + 1) * 128], ident[:])
                    return inst
                P.q("tensor", htr)
                P.q("vector", lambda v, pbt=pbt: v.tensor_copy(HIDC[:, :, :], pbt[:, 0:512]))
                for n in range(2):
                    P.qmm(PA[:, 0:512], [(HIDC[:, kb, :], WD[:, kb, n * 512:(n + 1) * 512])
                                         for kb in range(4)])
                    sl = slice(n * 512, (n + 1) * 512)
                    P.q("vector", lambda v, i=i, sl=sl, e=e: v.scalar_tensor_tensor(
                        XT[:, i, sl], PA[:, 0:512], CW[:, i, e:e + 1],
                        XT[:, i, sl], ALU.mult, ALU.add))

        for i in range(NT):
            pbt = C.psum("PBtr", [128, 1024], BF16)

            def ctr(t, i=i, pbt=pbt):
                inst = None
                for b in range(ND):
                    inst = t.transpose(pbt[:, b * 128:(b + 1) * 128],
                                       CONVC[:, b, i * 128:(i + 1) * 128], ident[:])
                return inst
            P.q("tensor", ctr)
            P.q("vector", lambda v, i=i, pbt=pbt: v.tensor_tensor(
                XT[:, i, :], XT[:, i, :], pbt[:, 0:1024], ALU.add))
            postln_tile(P, C, XT[:, i, :], STAT, SCR[:, :], eps=1e-5)
        P.qdma("sync", x_out[s].rearrange("(i p) d -> p i d", p=128), XT[:, :, :])
    P.finish()
    return names


BLOCKS = [build_L0a, build_L0f, build_L1a, build_L1f, build_L2a, build_L2f,
          build_L3a, build_L3f]

_cache = {}


def _get_program(bi):
    if bi not in _cache:
        import contextlib
        es = contextlib.ExitStack()
        nc = bass.Bass("TRN2", target_bir_lowering=False)
        with es:
            names = BLOCKS[bi](nc, es)
        _cache[bi] = (nc, names)
    return _cache[bi]


N_SUB = len(BLOCKS)


def kernel(**inputs):
    st = _stage(inputs)
    x = np.ascontiguousarray(np.asarray(inputs["L0_attn_x"], np.float32))
    shards = x.reshape(NCORES, S_PER_CORE, T, D)
    cur = [np.ascontiguousarray(shards[c]) for c in range(NCORES)]
    for bi in range(N_SUB):
        nc, names = _get_program(bi)
        in_maps = []
        for c in range(NCORES):
            m = {"x_in": cur[c]}
            for n in names:
                if n != "x_in":
                    m[n] = np.ascontiguousarray(st[n].astype(np.float32))
            in_maps.append(m)
        res = run_bass_kernel_spmd(nc, in_maps, core_ids=list(range(NCORES)))
        cur = [np.asarray(res.results[c]["x_out"]) for c in range(NCORES)]
    return np.concatenate(cur, axis=0).reshape(32, T, D)


# version 02 (final)
# speedup vs baseline: 1.1829x; 1.1829x over previous
import math
import numpy as np
import ml_dtypes
import concourse.bass as bass
import concourse.mybir as mybir
from concourse.bass_utils import run_bass_kernel_spmd

F32 = mybir.dt.float32
BF16 = mybir.dt.bfloat16
AF = mybir.ActivationFunctionType
ALU = mybir.AluOpType
AX = mybir.AxisListType
BF = ml_dtypes.bfloat16

NCORES = 8
S_PER_CORE = 4
T = 2048
D = 1024
NT = T // 128
ND = D // 128

ENGINES = ("tensor", "vector", "scalar", "gpsimd", "sync")



def _tile128(v):
    v = np.ascontiguousarray(np.asarray(v, np.float32))
    return np.ascontiguousarray(np.broadcast_to(v[None, :], (128, v.shape[0])))


CONST_VALS = [1e-5, 1e-6, 1e-9, 1e-20, 1e-12, 0.0, 1.0]
CONST_IDX = {v: i for i, v in enumerate(CONST_VALS)}


def _stage(ins):
    st = {}
    f32 = np.float32
    st["c_scalars"] = _tile128(np.array(CONST_VALS, f32))
    st["c_ident"] = np.eye(128, dtype=f32)
    m = np.zeros((128, 128), f32)
    m[np.triu_indices(128, k=1)] = -1e30
    st["c_causal0"] = m
    WqkvT = np.ascontiguousarray(np.asarray(ins["L0_attn_W_qkv"]).T)
    pair_cols = []
    for p in range(8):
        h0, h1 = 2 * p, 2 * p + 1
        cols = []
        for part in range(3):
            cols.append(WqkvT[:, 192 * h0 + 64 * part: 192 * h0 + 64 * (part + 1)])
            cols.append(WqkvT[:, 192 * h1 + 64 * part: 192 * h1 + 64 * (part + 1)])
        pair_cols.append(np.concatenate(cols, axis=1))
    st["L0_wqkv_pg"] = np.ascontiguousarray(np.stack(pair_cols))
    st["L0_wo_t"] = np.ascontiguousarray(np.asarray(ins["L0_attn_W_o"]).T)
    st["L0_gamma"] = _tile128(ins["L0_attn_ln_gamma"])
    st["L0_beta"] = _tile128(ins["L0_attn_ln_beta"])
    st["L0f_w_t"] = np.ascontiguousarray(np.asarray(ins["L0_ffn_weights"]).T)
    st["L0f_scale"] = _tile128(ins["L0_ffn_scale"])
    st["L0f_bias"] = _tile128(ins["L0_ffn_bias"])
    a = np.arange(128)
    keep = (a[None, :] < a[:, None]).astype(f32)
    st["c_strict_keep"] = keep
    st["c_strict_fill"] = (1.0 - keep) * np.float32(-1e6)
    Wq = np.asarray(ins["L1_attn_Wq"]); Wk = np.asarray(ins["L1_attn_Wk"]); Wv = np.asarray(ins["L1_attn_Wv"])
    st["L1_wqkv_pg"] = np.ascontiguousarray(np.stack([
        np.concatenate([Wq[:, p * 128:(p + 1) * 128],
                        Wk[:, p * 128:(p + 1) * 128],
                        Wv[:, p * 128:(p + 1) * 128]], axis=1) for p in range(8)]))
    st["L1_wo"] = np.ascontiguousarray(np.asarray(ins["L1_attn_Wo"]))
    st["L1_w1"] = np.ascontiguousarray(np.asarray(ins["L1_attn_W1"]))
    st["L1_w2"] = np.ascontiguousarray(np.asarray(ins["L1_attn_W2"]))
    st["L1_ln1"] = _tile128(ins["L1_attn_ln1_w"])
    st["L1_ln2"] = _tile128(ins["L1_attn_ln2_w"])
    st["L1_scaleh"] = _tile128(ins["L1_attn_scale"])
    NB, MAXD = 32, 128
    d = np.arange(-127, 128 * 15 + 128)
    rp = np.maximum(d, 0)
    safe = np.maximum(rp, 1).astype(np.float32)
    large = 16 + (np.log(safe / np.float32(16.0)) / np.float32(math.log(MAXD / 16)) * np.float32(16.0)).astype(np.int32)
    large = np.minimum(large, NB - 1)
    bucket_d = np.where(rp < 16, rp, large)
    rel = np.asarray(ins["L1_attn_rel_bias"])
    aa = np.arange(128)
    tiles = np.empty((16, 16, 128, 128), f32)
    for t in range(16):
        dmat = 128 * t + aa[:, None] - aa[None, :]
        bt = bucket_d[dmat + 127]
        tiles[:, t] = np.transpose(rel[bt], (2, 0, 1))
    st["L1_btiles"] = np.ascontiguousarray(tiles)
    st["c_iota32"] = _tile128(np.arange(32, dtype=f32))
    st["c_idxm32"] = _tile128(np.arange(32, dtype=f32) - np.float32(1e4))
    st["L1f_gate_wt"] = np.ascontiguousarray(np.asarray(ins["L1_ffn_gate_w"]).T)
    st["L1f_gate_bias"] = _tile128(ins["L1_ffn_gate_bias"])
    st["L1f_eg_t"] = np.ascontiguousarray(np.asarray(ins["L1_ffn_expert_gate"]).transpose(0, 2, 1))
    st["L1f_eu_t"] = np.ascontiguousarray(np.asarray(ins["L1_ffn_expert_up"]).transpose(0, 2, 1))
    st["L1f_ed_t"] = np.ascontiguousarray(np.asarray(ins["L1_ffn_expert_down"]).transpose(0, 2, 1))
    st["L1f_sg_t"] = np.ascontiguousarray(np.asarray(ins["L1_ffn_shared_gate"]).T)
    st["L1f_su_t"] = np.ascontiguousarray(np.asarray(ins["L1_ffn_shared_up"]).T)
    st["L1f_sd_t"] = np.ascontiguousarray(np.asarray(ins["L1_ffn_shared_down"]).T)
    Wq2 = np.asarray(ins["L2_attn_w_qkv"])
    st["L2_wqkv_pg"] = np.ascontiguousarray(np.stack([
        np.concatenate([Wq2[:, 128 * p:128 * p + 128],
                        Wq2[:, 1024 + 128 * p:1024 + 128 * p + 128],
                        Wq2[:, 2048 + 128 * p:2048 + 128 * p + 128]], axis=1)
        for p in range(8)]))
    st["L2_wout"] = np.ascontiguousarray(np.asarray(ins["L2_attn_w_out"]))
    wq3 = np.asarray(ins["L2_ffn_wqkv"])
    st["L2f_wqkv_pg"] = np.ascontiguousarray(np.stack([
        np.stack([np.concatenate([wq3[l, :, 128 * p:128 * p + 128],
                                  wq3[l, :, 512 + 128 * p:512 + 128 * p + 128],
                                  wq3[l, :, 1024 + 128 * p:1024 + 128 * p + 128]], axis=1)
                  for p in range(4)]) for l in range(2)]))
    st["L2f_wo"] = np.ascontiguousarray(np.asarray(ins["L2_ffn_wo"]))
    st["L2f_w1"] = np.ascontiguousarray(np.asarray(ins["L2_ffn_w1"]))
    st["L2f_w2"] = np.ascontiguousarray(np.asarray(ins["L2_ffn_w2"]))
    st["L2f_b1"] = np.ascontiguousarray(np.stack([_tile128(np.asarray(ins["L2_ffn_b1"])[l]) for l in range(2)]))
    st["L2f_b2"] = np.ascontiguousarray(np.stack([_tile128(np.asarray(ins["L2_ffn_b2"])[l]) for l in range(2)]))
    st["L2f_lng"] = np.ascontiguousarray(np.stack([_tile128(np.asarray(ins["L2_ffn_ln_g"])[l]) for l in range(2)]))
    st["L2f_lnb"] = np.ascontiguousarray(np.stack([_tile128(np.asarray(ins["L2_ffn_ln_b"])[l]) for l in range(2)]))
    st["L2f_qs"] = np.ascontiguousarray(np.stack([_tile128(np.tile(np.asarray(ins["L2_ffn_q_scale"])[l], 2)) for l in range(2)]))
    st["L2f_ks"] = np.ascontiguousarray(np.stack([_tile128(np.tile(np.asarray(ins["L2_ffn_k_scale"])[l], 2)) for l in range(2)]))
    DH = 64
    half_idx = np.arange(0, DH, 2, dtype=f32)
    inv_freq = (1.0 / np.power(np.float32(10000.0), half_idx / np.float32(DH))).astype(f32)
    tpos = np.arange(T, dtype=f32)
    freqs = np.concatenate([tpos[:, None] * inv_freq[None, :]] * 2, axis=1)
    base = ((half_idx + np.float32(0.4 * DH)) / np.float32(1.4 * DH)).astype(f32)
    power = ((tpos - np.float32(T // 2)) / np.float32(256.0)).astype(f32)
    xsc = np.concatenate([np.power(base[None, :], power[:, None]).astype(f32)] * 2, axis=1)
    cosf, sinf = np.cos(freqs).astype(f32), np.sin(freqs).astype(f32)
    def _rtile(tab):
        return np.ascontiguousarray(np.tile(tab.reshape(NT, 128, DH), (1, 1, 2)))
    st["L2f_cq"] = _rtile(cosf * xsc * np.float32(8.0))
    st["L2f_sq"] = _rtile(sinf * xsc * np.float32(8.0))
    st["L2f_ck"] = _rtile(cosf / xsc)
    st["L2f_sk"] = _rtile(sinf / xsc)
    jj = np.arange(1024)
    msk = np.zeros((4, 128, 1024), f32)
    for sdx in range(4):
        lo = 128 * sdx + np.arange(128)[:, None]
        bad = (jj[None, :] < lo) | (jj[None, :] > lo + 512)
        msk[sdx][bad] = -1e30
    st["c_l2f_mask"] = msk
    Wq3t = np.ascontiguousarray(np.asarray(ins["L3_attn_w_qkv"]).T)
    st["L3a_wqkv_pg"] = np.ascontiguousarray(np.stack([
        np.concatenate([Wq3t[:, 128 * p:128 * p + 128],
                        Wq3t[:, 1024 + 128 * p:1024 + 128 * p + 128],
                        Wq3t[:, 2048 + 128 * p:2048 + 128 * p + 128]], axis=1)
        for p in range(8)]))
    st["L3a_wproj_t"] = np.ascontiguousarray(np.asarray(ins["L3_attn_w_proj"]).T)
    st["L3a_bproj"] = _tile128(ins["L3_attn_b_proj"])
    st["L3a_wdwc"] = np.ascontiguousarray(np.asarray(ins["L3_attn_w_dwc"]).reshape(8, 128, 3).transpose(1, 0, 2))
    st["L3a_bdwc"] = np.ascontiguousarray(np.asarray(ins["L3_attn_b_dwc"]).reshape(8, 128).T)
    st["c_ones"] = np.ones((128, 128), f32)
    st["L3f_wr"] = np.ascontiguousarray(np.asarray(ins["L3_ffn_Wr"]))
    st["L3f_rbias"] = _tile128(ins["L3_ffn_router_bias"])
    st["L3f_wg"] = np.ascontiguousarray(np.asarray(ins["L3_ffn_Wg"]))
    st["L3f_wu"] = np.ascontiguousarray(np.asarray(ins["L3_ffn_Wu"]))
    st["L3f_wd"] = np.ascontiguousarray(np.asarray(ins["L3_ffn_Wd"]))
    st["L3f_convw"] = np.ascontiguousarray(np.asarray(ins["L3_ffn_conv_w"])[:, 0, :].T.reshape(8, 128, 4).transpose(1, 0, 2))
    cw4 = np.asarray(ins["L3_ffn_conv_w"])[:, 0, :]
    st["L3f_convw"] = np.ascontiguousarray(cw4.T.reshape(8, 128, 4).transpose(1, 0, 2))
    st["L3f_convb"] = np.ascontiguousarray(np.asarray(ins["L3_ffn_conv_b"]).reshape(8, 128).T)
    return st



class Prog:

    def __init__(self, nc, es):
        self.nc = nc
        self.es = es
        self.items = {e: [] for e in ENGINES}
        self.sems = {}
        self.cnt = {}
        self.sem("Q")

    def sem(self, name):
        if name not in self.sems:
            self.sems[name] = self.es.enter_context(self.nc.semaphore(name))
            self.cnt[name] = 0
        return name

    def val(self, name):
        return self.cnt[name]

    def emit(self, eng, fn, waits=(), inc=None, inc_amt=1):
        if inc is not None:
            self.sem(inc)
            self.cnt[inc] += inc_amt
        self.items[eng].append((fn, [(s, v) for s, v in waits], inc, inc_amt))

    def q(self, eng, fn):
        self.emit(eng, fn, waits=[("Q", self.val("Q"))], inc="Q", inc_amt=1)

    def qdma(self, eng, out_ap, in_ap):
        self.emit(eng, lambda e, o=out_ap, i=in_ap: e.dma_start(o, i),
                  waits=[("Q", self.val("Q"))], inc="Q", inc_amt=16)

    def qmm(self, psum_ap, pairs, tile_position=None):
        n = len(pairs)

        def fn(t, psum_ap=psum_ap, pairs=pairs, tp=tile_position):
            inst = None
            for j, (l, r) in enumerate(pairs):
                inst = t.matmul(psum_ap, l, r, start=(j == 0), stop=(j == n - 1),
                                tile_position=tp)
            return inst
        self.q("tensor", fn)

    def finish(self):
        self.emit("sync", lambda e: e.nop(), waits=[("Q", self.val("Q"))])
        with self.nc.Block() as blk:
            for eng in ENGINES:
                items = self.items[eng]
                if not items:
                    continue

                def body(e, items=items):
                    for fn, waits, inc, amt in items:
                        for s, v in waits:
                            if v > 0:
                                e.wait_ge(self.sems[s], v)
                        inst = fn(e)
                        if inc is not None:
                            assert inst is not None, "op with inc returned no instruction"
                            inst.then_inc(self.sems[inc], amt)
                getattr(blk, eng)(body)


class Ctx:
    def __init__(self, P, es=None, pfx="", shared=None):
        self.P = P
        self.es = es if es is not None else P.es
        self.pfx = pfx
        self.t = {}
        if shared:
            self.t.update(shared)

    def sbuf(self, name, shape, dtype):
        if name not in self.t:
            self.t[name] = self.es.enter_context(
                self.P.nc.sbuf_tensor(self.pfx + name, shape, dtype))
        return self.t[name]

    def psum(self, name, shape, dtype):
        if name not in self.t:
            self.t[name] = self.es.enter_context(
                self.P.nc.psum_tensor(self.pfx + name, shape, dtype))
        return self.t[name]

    def cs(self, val):
        return self.t["CONSTS"][:, CONST_IDX[val]:CONST_IDX[val] + 1]



def postln_tile(P, C, xt_tile, stat, scr, eps=1e-5, gamma=None, beta=None, out_ap=None):
    P.q("vector", lambda v: v.reduce_sum(stat[:, 0:1], xt_tile, axis=AX.X, negate=True))
    P.q("vector", lambda v: v.tensor_scalar(stat[:, 1:2], stat[:, 0:1], 1.0 / D, None, ALU.mult))
    P.q("scalar", lambda s: s.activation(scr, xt_tile, AF.Square,
                                         bias=stat[:, 1:2], scale=1.0,
                                         accum_out=stat[:, 2:3]))
    P.q("scalar", lambda s: s.activation(stat[:, 3:4], stat[:, 2:3], AF.Sqrt,
                                         bias=C.cs(eps), scale=1.0 / D))
    P.q("vector", lambda v: v.reciprocal(stat[:, 4:5], stat[:, 3:4]))
    if gamma is None:
        dst = xt_tile if out_ap is None else out_ap
        P.q("vector", lambda v: v.tensor_scalar(dst, xt_tile, stat[:, 1:2], stat[:, 4:5],
                                                ALU.add, ALU.mult))
    else:
        P.q("vector", lambda v: v.tensor_scalar(scr, xt_tile, stat[:, 1:2], stat[:, 4:5],
                                                ALU.add, ALU.mult))
        P.q("vector", lambda v: v.tensor_tensor(scr, scr, gamma, ALU.mult))
        P.q("gpsimd", lambda g: g.tensor_tensor(out_ap, scr, beta, ALU.add))


def tok_to_ch(P, C, src_bf_tile, dst_ap, nd=ND):
    pb = C.psum("PBtr", [128, 1024], BF16)

    def trs(t):
        inst = None
        for k in range(nd):
            inst = t.transpose(pb[:, k * 128:(k + 1) * 128],
                               src_bf_tile[:, k * 128:(k + 1) * 128], ident_of(C))
        return inst
    P.q("tensor", trs)
    P.q("vector", lambda v: v.tensor_copy(dst_ap, pb[:, 0:nd * 128]))


def ident_of(C):
    return C.t["ident"][:, :]



WSHAPES = {
    "c_scalars": [128, len(CONST_VALS)], "c_ident": [128, 128],
    "c_causal0": [128, 128], "c_strict_keep": [128, 128],
    "c_strict_fill": [128, 128], "c_iota32": [128, 32], "c_idxm32": [128, 32],
    "c_ones": [128, 128], "c_l2f_mask": [4, 128, 1024],
    "L0_wqkv_pg": [8, D, 384], "L0_wo_t": [D, D],
    "L0_gamma": [128, D], "L0_beta": [128, D],
    "L0f_w_t": [D, D], "L0f_scale": [128, D], "L0f_bias": [128, D],
    "L1_wqkv_pg": [8, D, 384], "L1_wo": [D, D], "L1_w1": [D, 4096],
    "L1_w2": [4096, D], "L1_ln1": [128, D], "L1_ln2": [128, D],
    "L1_scaleh": [128, 16], "L1_btiles": [16, 16, 128, 128],
    "L1f_gate_wt": [D, 32], "L1f_gate_bias": [128, 32],
    "L1f_eg_t": [32, D, 512], "L1f_eu_t": [32, D, 512], "L1f_ed_t": [32, 512, D],
    "L1f_sg_t": [D, D], "L1f_su_t": [D, D], "L1f_sd_t": [D, D],
    "L2_wqkv_pg": [8, D, 384], "L2_wout": [D, D],
    "L2f_wqkv_pg": [2, 4, D, 384], "L2f_wo": [2, 512, D],
    "L2f_w1": [2, D, 5460], "L2f_w2": [2, 2730, D],
    "L2f_b1": [2, 128, 5460], "L2f_b2": [2, 128, D],
    "L2f_lng": [2, 128, D], "L2f_lnb": [2, 128, D],
    "L2f_qs": [2, 128, 128], "L2f_ks": [2, 128, 128],
    "L2f_cq": [NT, 128, 128], "L2f_sq": [NT, 128, 128],
    "L2f_ck": [NT, 128, 128], "L2f_sk": [NT, 128, 128],
    "L3a_wqkv_pg": [8, D, 384], "L3a_wproj_t": [D, D], "L3a_bproj": [128, D],
    "L3a_wdwc": [128, 8, 3], "L3a_bdwc": [128, 8],
    "L3f_wr": [D, 8], "L3f_rbias": [128, 8],
    "L3f_wg": [4, D, 512], "L3f_wu": [4, D, 512], "L3f_wd": [4, 512, D],
    "L3f_convw": [128, 8, 4], "L3f_convb": [128, 8],
}


def build_L0a(nc, es, P, x_in, x_out, w, shared):
    C = Ctx(P, es, "B0_", shared)
    causal = C.t["causal"]
    gamma = C.sbuf("gamma", [128, D], F32)
    beta = C.sbuf("beta", [128, D], F32)
    XT = C.sbuf("XT", [128, NT, D], F32)
    XN = C.sbuf("XN", [128, D], BF16)
    XN_C = C.sbuf("XN_C", [128, ND, T], BF16)
    WP = C.sbuf("WP", [128, ND, 384], BF16)
    WO = C.sbuf("WO", [128, ND, D], BF16)
    QPC = C.sbuf("QPC", [128, T], BF16)
    KPC = C.sbuf("KPC", [128, T], BF16)
    VPT = C.sbuf("VPT", [128, NT, 128], BF16)
    SS = C.sbuf("SS", [128, T], F32)
    PP = C.sbuf("PP", [128, T], BF16)
    PT = C.sbuf("PT", [128, T], BF16)
    CTX = C.sbuf("CTX", [128, ND, T], BF16)
    STAT = C.sbuf("STAT", [128, 64], F32)
    SCR = C.sbuf("SCR", [128, D], F32)

    PA = C.psum("PA", [128, 512], F32)
    PB2 = C.psum("PB2", [128, 512], F32)
    PC = C.psum("PC", [128, 128], F32)

    ident = C.t["ident"]
    P.qdma("gpsimd", gamma[:], w["L0_gamma"])
    P.qdma("gpsimd", beta[:], w["L0_beta"])
    P.qdma("gpsimd", WO[:, :, :], w["L0_wo_t"].rearrange("(b p) n -> p b n", p=128))

    for s in range(S_PER_CORE):
        P.qdma("sync", XT[:, :, :], x_in[s].rearrange("(i p) d -> p i d", p=128))

        for i in range(NT):
            postln_tile(P, C, XT[:, i, :], STAT, SCR[:, :], eps=1e-5,
                        gamma=gamma[:], beta=beta[:], out_ap=XN[:])
            tok_to_ch(P, C, XN[:], XN_C[:, :, i * 128:(i + 1) * 128])

        for p in range(8):
            P.qdma("gpsimd", WP[:, :, :], w["L0_wqkv_pg"][p].rearrange("(b p) n -> p b n", p=128))
            for dst, col0 in ((QPC, 0), (KPC, 128)):
                for c in range(4):
                    P.qmm(PA[:, 0:512],
                          [(WP[:, k, col0:col0 + 128], XN_C[:, k, c * 512:(c + 1) * 512])
                           for k in range(ND)])
                    P.q("vector", lambda v, dst=dst, c=c: v.tensor_copy(
                        dst[:, c * 512:(c + 1) * 512], PA[:, 0:512]))
            for i in range(NT):
                P.qmm(PC[:, 0:128],
                      [(XN_C[:, k, i * 128:(i + 1) * 128], WP[:, k, 256:384])
                       for k in range(ND)])
                P.q("scalar", lambda sc, i=i: sc.copy(VPT[:, i, :], PC[:, 0:128]))

            for hh in range(2):
                qsl = slice(64 * hh, 64 * hh + 64)
                for i in range(NT):
                    nkt = i + 1
                    nk = nkt * 128
                    for c in range((nk + 511) // 512):
                        k0, k1 = c * 512, min(nk, (c + 1) * 512)
                        pa = PA if c % 2 == 0 else PB2
                        P.qmm(pa[:, 0:k1 - k0],
                              [(QPC[qsl, i * 128:(i + 1) * 128], KPC[qsl, k0:k1])])
                        if k1 == nk:
                            if k1 - k0 > 128:
                                P.q("vector", lambda v, k0=k0, k1=k1, pa=pa: v.tensor_copy(
                                    SS[:, k0:k1 - 128], pa[:, 0:k1 - k0 - 128]))
                            P.q("vector", lambda v, k0=k0, k1=k1, pa=pa: v.tensor_tensor(
                                SS[:, k1 - 128:k1], pa[:, k1 - k0 - 128:k1 - k0],
                                causal[:], ALU.add))
                        else:
                            P.q("vector", lambda v, k0=k0, k1=k1, pa=pa: v.tensor_copy(
                                SS[:, k0:k1], pa[:, 0:k1 - k0]))
                    P.q("vector", lambda v, nk=nk: v.reduce_max(
                        STAT[:, 8:9], SS[:, 0:nk], axis=AX.X, negate=True))
                    P.q("vector", lambda v: v.tensor_scalar(
                        STAT[:, 9:10], STAT[:, 8:9], 0.125, None, ALU.mult))
                    P.q("scalar", lambda sc, nk=nk: sc.activation(
                        PP[:, 0:nk], SS[:, 0:nk], AF.Exp, bias=STAT[:, 9:10], scale=0.125,
                        accum_out=STAT[:, 10:11]))
                    P.q("vector", lambda v: v.reciprocal(STAT[:, 11:12], STAT[:, 10:11]))
                    P.q("vector", lambda v, nk=nk: v.tensor_scalar(
                        PP[:, 0:nk], PP[:, 0:nk], STAT[:, 11:12], None, ALU.mult))
                    for r0 in range(0, nkt, 8):
                        r1 = min(nkt, r0 + 8)
                        pbt = C.psum("PBtr", [128, 1024], BF16)

                        def ptr(t, r0=r0, r1=r1, pbt=pbt):
                            inst = None
                            for j in range(r0, r1):
                                inst = t.transpose(pbt[:, (j - r0) * 128:(j - r0 + 1) * 128],
                                                   PP[:, j * 128:(j + 1) * 128], ident[:])
                            return inst
                        P.q("tensor", ptr)
                        P.q("vector", lambda v, r0=r0, r1=r1, pbt=pbt: v.tensor_copy(
                            PT[:, r0 * 128:r1 * 128], pbt[:, 0:(r1 - r0) * 128]))
                    tp = (0, 64 * hh)
                    P.qmm(PC[64 * hh:64 * hh + 64, 0:128],
                          [(VPT[:, j, qsl], PT[:, j * 128:(j + 1) * 128]) for j in range(nkt)],
                          tile_position=tp)
                    P.q("scalar", lambda sc, p=p, i=i, hh=hh: sc.copy(
                        CTX[64 * hh:64 * hh + 64, p, i * 128:(i + 1) * 128],
                        PC[64 * hh:64 * hh + 64, 0:128]))

        for i in range(NT):
            for n in range(2):
                P.qmm(PA[:, 0:512],
                      [(CTX[:, k, i * 128:(i + 1) * 128], WO[:, k, n * 512:(n + 1) * 512])
                       for k in range(ND)])
                P.q("vector", lambda v, i=i, n=n: v.tensor_tensor(
                    XT[:, i, n * 512:(n + 1) * 512], PA[:, 0:512],
                    XT[:, i, n * 512:(n + 1) * 512], ALU.add))
            postln_tile(P, C, XT[:, i, :], STAT, SCR[:, :], eps=1e-5)
        P.qdma("sync", x_out[s].rearrange("(i p) d -> p i d", p=128), XT[:, :, :])


def build_L0f(nc, es, P, x_in, x_out, w, shared):
    C = Ctx(P, es, "B1_", shared)
    ident = C.t["ident"]
    scale = C.sbuf("scale", [128, D], F32)
    bias = C.sbuf("bias", [128, D], F32)
    XT = C.sbuf("XT", [128, NT, D], F32)
    XB = C.sbuf("XB", [128, D], BF16)
    XC = C.sbuf("XC", [128, ND, T], BF16)
    SW = C.sbuf("SW", [128, ND, D], BF16)
    STAT = C.sbuf("STAT", [128, 64], F32)
    SCR = C.sbuf("SCR", [128, D], F32)
    PA = C.psum("PA", [128, 512], F32)

    P.qdma("gpsimd", scale[:], w["L0f_scale"])
    P.qdma("gpsimd", bias[:], w["L0f_bias"])
    P.qdma("gpsimd", SW[:, :, :], w["L0f_w_t"].rearrange("(b p) n -> p b n", p=128))
    P.q("vector", lambda v: v.tensor_scalar(SW.ap().rearrange("p a b -> p (a b)"),
                                            SW.ap().rearrange("p a b -> p (a b)"),
                                            0.0, None, ALU.is_ge))
    P.q("vector", lambda v: v.tensor_scalar(SW.ap().rearrange("p a b -> p (a b)"),
                                            SW.ap().rearrange("p a b -> p (a b)"),
                                            2.0, 1.0, ALU.mult, ALU.subtract))

    for s in range(S_PER_CORE):
        P.qdma("sync", XT[:, :, :], x_in[s].rearrange("(i p) d -> p i d", p=128))
        for i in range(NT):
            P.q("vector", lambda v, i=i: v.tensor_copy(XB[:], XT[:, i, :]))
            tok_to_ch(P, C, XB[:], XC[:, :, i * 128:(i + 1) * 128])
        for i in range(NT):
            for n in range(2):
                P.qmm(PA[:, 0:512],
                      [(XC[:, k, i * 128:(i + 1) * 128], SW[:, k, n * 512:(n + 1) * 512])
                       for k in range(ND)])
                sl = slice(n * 512, (n + 1) * 512)
                P.q("vector", lambda v, i=i, sl=sl: v.tensor_tensor(
                    SCR[:, sl], PA[:, 0:512], scale[:, sl], ALU.mult))
                P.q("vector", lambda v, i=i, sl=sl: v.tensor_tensor(
                    SCR[:, sl], SCR[:, sl], bias[:, sl], ALU.add))
                P.q("vector", lambda v, i=i, sl=sl: v.tensor_scalar(
                    SCR[:, sl], SCR[:, sl], 100.0, -100.0, ALU.min, ALU.max))
                P.q("vector", lambda v, i=i, sl=sl: v.tensor_tensor(
                    XT[:, i, sl], XT[:, i, sl], SCR[:, sl], ALU.add))
            postln_tile(P, C, XT[:, i, :], STAT, SCR[:, :], eps=1e-5)
        P.qdma("sync", x_out[s].rearrange("(i p) d -> p i d", p=128), XT[:, :, :])


def build_L1a(nc, es, P, x_in, x_out, w, shared):
    C = Ctx(P, es, "B2_", shared)
    ident = C.t["ident"]
    mkeep = C.t["mkeep"]
    mfill = C.t["mfill"]
    ln1 = C.sbuf("ln1", [128, D], BF16)
    ln2 = C.sbuf("ln2", [128, D], BF16)
    scaleh = C.sbuf("scaleh", [128, 16], F32)
    XT = C.sbuf("XT", [128, NT, D], F32)
    XB = C.sbuf("XB", [128, D], BF16)
    STAT = C.sbuf("STAT", [128, 64], F32)
    SCR = C.sbuf("SCR", [128, D], F32)
    ARE = C.sbuf("ARE", [128, 128 * 512], BF16)

    def V(off_kb, shape, dtype=BF16):
        esz = 2 if dtype == BF16 else 4
        n = int(np.prod(shape[1:]))
        ap = ARE.ap()[:, off_kb * 512: off_kb * 512 + (n * esz) // 2]
        if dtype == F32:
            ap = ap.bitcast(F32)
        if len(shape) == 3:
            ap = ap.rearrange("p (a b) -> p a b", a=shape[1])
        return ap

    XN_C = V(0, [128, ND, T])
    CTX = V(32, [128, ND, T])
    WP = V(64, [128, ND, 384])
    QTP = V(70, [128, NT, 128])
    KTP = V(74, [128, NT, 128])
    VPT = V(78, [128, NT, 128])
    QPC = V(82, [128, T])
    KPC = V(86, [128, T])
    PP = V(90, [128, T])
    PT = V(94, [128, T])
    BH = V(98, [128, NT, 128], F32)
    SS = V(106, [128, T], F32)
    YT = V(0, [128, NT, D])
    WO = V(64, [128, ND, D])
    ZC = V(32, [128, ND, T])
    W1Q = V(64, [128, ND, 1024])
    W2Q = V(80, [128, ND, 1024])
    H1 = V(96, [128, 1024])
    H1C = V(100, [128, ND, 128])

    PA = C.psum("PA", [128, 512], F32)
    PB2 = C.psum("PB2", [128, 512], F32)
    PC = C.psum("PC", [128, 128], F32)

    P.qdma("gpsimd", ln1[:], w["L1_ln1"])
    P.qdma("gpsimd", ln2[:], w["L1_ln2"])
    P.qdma("gpsimd", scaleh[:], w["L1_scaleh"])

    for s in range(S_PER_CORE):
        P.qdma("sync", XT[:, :, :], x_in[s].rearrange("(i p) d -> p i d", p=128))

        for i in range(NT):
            P.q("scalar", lambda sc, i=i: sc.activation(SCR[:], XT[:, i, :], AF.Square,
                                                        accum_out=STAT[:, 0:1]))
            P.q("scalar", lambda sc: sc.activation(STAT[:, 1:2], STAT[:, 0:1], AF.Sqrt,
                                                   bias=C.cs(1e-6), scale=1.0 / D))
            P.q("vector", lambda v: v.reciprocal(STAT[:, 2:3], STAT[:, 1:2]))
            P.q("vector", lambda v, i=i: v.tensor_scalar(SCR[:], XT[:, i, :],
                                                         STAT[:, 2:3], None, ALU.mult))
            P.q("vector", lambda v: v.tensor_tensor(XB[:], SCR[:], ln1[:], ALU.mult))
            tok_to_ch(P, C, XB[:], XN_C[:, :, i * 128:(i + 1) * 128])

        for p in range(8):
            P.qdma("gpsimd", WP, w["L1_wqkv_pg"][p].rearrange("(b p) n -> p b n", p=128))
            for which, dstT, col0 in (("q", QTP, 0), ("k", KTP, 128), ("v", VPT, 256)):
                for i in range(NT):
                    P.qmm(PC[:, 0:128],
                          [(XN_C[:, k, i * 128:(i + 1) * 128], WP[:, k, col0:col0 + 128])
                           for k in range(ND)])
                    if which == "v":
                        P.q("scalar", lambda sc, i=i, dstT=dstT: sc.copy(
                            dstT[:, i, :], PC[:, 0:128]))
                    else:
                        for hh in range(2):
                            hs = slice(64 * hh, 64 * hh + 64)
                            P.q("scalar", lambda sc, hs=hs: sc.activation(
                                SCR[:, 0:64], PC[:, hs], AF.Square, accum_out=STAT[:, 4:5]))
                            P.q("scalar", lambda sc: sc.activation(
                                STAT[:, 5:6], STAT[:, 4:5], AF.Sqrt, bias=C.cs(1e-6), scale=1.0))
                            P.q("vector", lambda v: v.reciprocal(STAT[:, 6:7], STAT[:, 5:6]))
                            P.q("vector", lambda v, i=i, hs=hs, dstT=dstT: v.tensor_scalar(
                                dstT[:, i, hs], PC[:, hs], STAT[:, 6:7], None, ALU.mult))
            for i in range(NT):
                pbt = C.psum("PBtr", [128, 1024], BF16)
                P.q("tensor", lambda t, i=i, pbt=pbt: t.transpose(pbt[:, 0:128], QTP[:, i, :], ident[:]))
                P.q("vector", lambda v, i=i, pbt=pbt: v.tensor_copy(QPC[:, i * 128:(i + 1) * 128],
                                                                    pbt[:, 0:128]))
                P.q("tensor", lambda t, i=i, pbt=pbt: t.transpose(pbt[:, 128:256], KTP[:, i, :], ident[:]))
                P.q("vector", lambda v, i=i, pbt=pbt: v.tensor_copy(KPC[:, i * 128:(i + 1) * 128],
                                                                    pbt[:, 128:256]))

            for hh in range(2):
                h = 2 * p + hh
                P.qdma("gpsimd", BH, w["L1_btiles"][h].rearrange("t p b -> p t b"))
                qsl = slice(64 * hh, 64 * hh + 64)
                for i in range(NT):
                    nkt_c = i + 1
                    nk = 2048 if i == 0 else nkt_c * 128
                    nkt_av = 16 if i == 0 else nkt_c
                    for c in range((nkt_c * 128 + 511) // 512):
                        k0, k1 = c * 512, min(nkt_c * 128, (c + 1) * 512)
                        pa = PA if c % 2 == 0 else PB2
                        P.qmm(pa[:, 0:k1 - k0],
                              [(QPC[qsl, i * 128:(i + 1) * 128], KPC[qsl, k0:k1])])
                        for j in range(k0 // 128, k1 // 128):
                            P.q("vector", lambda v, j=j, k0=k0, pa=pa, i=i: v.tensor_tensor(
                                SS[:, j * 128:(j + 1) * 128],
                                pa[:, j * 128 - k0:(j + 1) * 128 - k0],
                                BH[:, i - j, :], ALU.add))
                    P.q("vector", lambda v, nk2=nkt_c * 128, h=h: v.tensor_scalar(
                        SS[:, 0:nk2], SS[:, 0:nk2], scaleh[:, h:h + 1], None, ALU.mult))
                    dof = (nkt_c - 1) * 128
                    P.q("vector", lambda v, dof=dof: v.tensor_tensor(
                        SS[:, dof:dof + 128], SS[:, dof:dof + 128], mkeep[:], ALU.mult))
                    P.q("vector", lambda v, dof=dof: v.tensor_tensor(
                        SS[:, dof:dof + 128], SS[:, dof:dof + 128], mfill[:], ALU.add))
                    if i == 0:
                        P.q("gpsimd", lambda g: g.memset(SS[:, 128:2048], -1e6))
                    P.q("vector", lambda v, nk=nk: v.reduce_max(
                        STAT[:, 8:9], SS[:, 0:nk], axis=AX.X, negate=True))
                    P.q("scalar", lambda sc, nk=nk: sc.activation(
                        PP[:, 0:nk], SS[:, 0:nk], AF.Exp, bias=STAT[:, 8:9], scale=1.0,
                        accum_out=STAT[:, 10:11]))
                    P.q("vector", lambda v: v.reciprocal(STAT[:, 11:12], STAT[:, 10:11]))
                    P.q("vector", lambda v, nk=nk: v.tensor_scalar(
                        PP[:, 0:nk], PP[:, 0:nk], STAT[:, 11:12], None, ALU.mult))
                    for r0 in range(0, nkt_av, 8):
                        r1 = min(nkt_av, r0 + 8)
                        pbt = C.psum("PBtr", [128, 1024], BF16)

                        def ptr(t, r0=r0, r1=r1, pbt=pbt):
                            inst = None
                            for j in range(r0, r1):
                                inst = t.transpose(pbt[:, (j - r0) * 128:(j - r0 + 1) * 128],
                                                   PP[:, j * 128:(j + 1) * 128], ident[:])
                            return inst
                        P.q("tensor", ptr)
                        P.q("vector", lambda v, r0=r0, r1=r1, pbt=pbt: v.tensor_copy(
                            PT[:, r0 * 128:r1 * 128], pbt[:, 0:(r1 - r0) * 128]))
                    P.qmm(PC[64 * hh:64 * hh + 64, 0:128],
                          [(VPT[:, j, qsl], PT[:, j * 128:(j + 1) * 128]) for j in range(nkt_av)],
                          tile_position=(0, 64 * hh))
                    P.q("scalar", lambda sc, p=p, i=i, hh=hh: sc.copy(
                        CTX[64 * hh:64 * hh + 64, p, i * 128:(i + 1) * 128],
                        PC[64 * hh:64 * hh + 64, 0:128]))

        P.qdma("gpsimd", WO, w["L1_wo"].rearrange("(b p) n -> p b n", p=128))
        for i in range(NT):
            for n in range(2):
                P.qmm(PA[:, 0:512],
                      [(CTX[:, k, i * 128:(i + 1) * 128], WO[:, k, n * 512:(n + 1) * 512])
                       for k in range(ND)])
                sl = slice(n * 512, (n + 1) * 512)
                P.q("vector", lambda v, i=i, sl=sl: v.tensor_tensor(
                    YT[:, i, sl], PA[:, 0:512], XT[:, i, sl], ALU.add))
                P.q("vector", lambda v, i=i, sl=sl: v.tensor_tensor(
                    XT[:, i, sl], XT[:, i, sl], YT[:, i, sl], ALU.add))

        for i in range(NT):
            P.q("scalar", lambda sc, i=i: sc.activation(SCR[:], YT[:, i, :], AF.Square,
                                                        accum_out=STAT[:, 0:1]))
            P.q("scalar", lambda sc: sc.activation(STAT[:, 1:2], STAT[:, 0:1], AF.Sqrt,
                                                   bias=C.cs(1e-6), scale=1.0 / D))
            P.q("vector", lambda v: v.reciprocal(STAT[:, 2:3], STAT[:, 1:2]))
            P.q("vector", lambda v, i=i: v.tensor_scalar(SCR[:], YT[:, i, :],
                                                         STAT[:, 2:3], None, ALU.mult))
            P.q("vector", lambda v: v.tensor_tensor(XB[:], SCR[:], ln2[:], ALU.mult))
            tok_to_ch(P, C, XB[:], ZC[:, :, i * 128:(i + 1) * 128])

        for qtr in range(4):
            P.qdma("gpsimd", W1Q,
                   w["L1_w1"][:, qtr * 1024:(qtr + 1) * 1024].rearrange("(b p) n -> p b n", p=128))
            P.qdma("gpsimd", W2Q,
                   w["L1_w2"][qtr * 1024:(qtr + 1) * 1024, :].rearrange("(b p) n -> p b n", p=128))
            for i in range(NT):
                for c in range(2):
                    P.qmm(PA[:, 0:512],
                          [(ZC[:, k, i * 128:(i + 1) * 128], W1Q[:, k, c * 512:(c + 1) * 512])
                           for k in range(ND)])
                    P.q("scalar", lambda sc, c=c: sc.activation(
                        H1[:, c * 512:(c + 1) * 512], PA[:, 0:512], AF.Relu))
                pbt = C.psum("PBtr", [128, 1024], BF16)

                def htr(t, pbt=pbt):
                    inst = None
                    for j in range(8):
                        inst = t.transpose(pbt[:, j * 128:(j + 1) * 128],
                                           H1[:, j * 128:(j + 1) * 128], ident[:])
                    return inst
                P.q("tensor", htr)
                P.q("vector", lambda v, pbt=pbt: v.tensor_copy(
                    H1C[:, :, :], pbt[:, 0:1024]))
                for n in range(2):
                    P.qmm(PA[:, 0:512],
                          [(H1C[:, kb, :], W2Q[:, kb, n * 512:(n + 1) * 512])
                           for kb in range(ND)])
                    sl = slice(n * 512, (n + 1) * 512)
                    P.q("vector", lambda v, i=i, sl=sl: v.tensor_tensor(
                        XT[:, i, sl], XT[:, i, sl], PA[:, 0:512], ALU.add))

        for i in range(NT):
            postln_tile(P, C, XT[:, i, :], STAT, SCR[:, :], eps=1e-5)
        P.qdma("sync", x_out[s].rearrange("(i p) d -> p i d", p=128), XT[:, :, :])


def argmax_first(P, vals_ap, iot, idxm, eq_out, tmp_a, tmp_b):
    P.q("vector", lambda v: v.reduce_max(tmp_b[:, 0:1], vals_ap, axis=AX.X))
    P.q("vector", lambda v: v.tensor_scalar(tmp_a, vals_ap, tmp_b[:, 0:1], None, ALU.is_equal))
    P.q("vector", lambda v: v.tensor_tensor(tmp_a, tmp_a, idxm, ALU.mult))
    P.q("vector", lambda v: v.tensor_reduce(tmp_b[:, 1:2], tmp_a, axis=AX.X, op=ALU.min))
    P.q("vector", lambda v: v.tensor_scalar(tmp_b[:, 1:2], tmp_b[:, 1:2], 1e4, None, ALU.add))
    P.q("vector", lambda v: v.tensor_scalar(eq_out, iot, tmp_b[:, 1:2], None, ALU.is_equal))


def build_L1f(nc, es, P, x_in, x_out, w, shared):
    C = Ctx(P, es, "B3_", shared)
    ident = C.t["ident"]
    iota32 = C.t["iota32"]
    idxm32 = C.t["idxm32"]
    gwt = C.sbuf("gwt", [128, ND, 32], BF16)
    gbias = C.sbuf("gbias", [128, 32], F32)
    XT = C.sbuf("XT", [128, NT, D], F32)
    XB = C.sbuf("XB", [128, D], BF16)
    XC = C.sbuf("XC", [128, ND, T], BF16)
    ACC = C.sbuf("ACC", [128, NT, D], F32)
    CW = C.sbuf("CW", [128, NT, 32], F32)
    RT = C.sbuf("RT", [128, 256], F32)
    WG = C.sbuf("WG", [128, ND, 512], BF16)
    WU = C.sbuf("WU", [128, ND, 512], BF16)
    WD = C.sbuf("WD", [128, 4, D], BF16)
    HG = C.sbuf("HG", [128, 512], BF16)
    HID = C.sbuf("HID", [128, 512], BF16)
    HIDC = C.sbuf("HIDC", [128, 4, 128], BF16)
    STAT = C.sbuf("STAT", [128, 64], F32)
    SCR = C.sbuf("SCR", [128, D], F32)
    PA = C.psum("PA", [128, 512], F32)
    PB2 = C.psum("PB2", [128, 512], F32)

    P.qdma("gpsimd", gwt[:, :, :], w["L1f_gate_wt"].rearrange("(b p) n -> p b n", p=128))
    P.qdma("gpsimd", gbias[:], w["L1f_gate_bias"])

    for s in range(S_PER_CORE):
        P.qdma("sync", XT[:, :, :], x_in[s].rearrange("(i p) d -> p i d", p=128))
        for i in range(NT):
            P.q("vector", lambda v, i=i: v.tensor_copy(XB[:], XT[:, i, :]))
            tok_to_ch(P, C, XB[:], XC[:, :, i * 128:(i + 1) * 128])

        SC = RT[:, 0:32]; CH = RT[:, 32:64]; CUR = RT[:, 64:96]; SELM = RT[:, 96:128]
        EQ = RT[:, 128:160]; TMP = RT[:, 160:192]; TB = RT[:, 192:200]
        GS = RT[:, 200:204]; GT2 = RT[:, 204:208]; GSUM = RT[:, 208:212]
        GEQ = RT[:, 212:216]; GMASK = RT[:, 216:220]
        for i in range(NT):
            P.qmm(PA[:, 0:32], [(XC[:, k, i * 128:(i + 1) * 128], gwt[:, k, :])
                                for k in range(ND)])
            P.q("scalar", lambda sc: sc.activation(SC, PA[:, 0:32], AF.Sigmoid))
            P.q("vector", lambda v: v.tensor_tensor(CH, SC, gbias[:], ALU.add))
            for g in range(4):
                sl = slice(8 * g, 8 * g + 8)
                argmax_first(P, RT[:, 32 + 8 * g:40 + 8 * g],
                             iota32[:, 0:8], idxm32[:, 0:8], EQ[:, 0:8], TMP[:, 0:8], TB)
                P.q("vector", lambda v, g=g: v.tensor_copy(GS[:, g:g + 1], TB[:, 0:1]))
                P.q("vector", lambda v, g=g: v.scalar_tensor_tensor(
                    TMP[:, 0:8], EQ[:, 0:8], -1e30, RT[:, 32 + 8 * g:40 + 8 * g],
                    ALU.mult, ALU.add))
                P.q("vector", lambda v, g=g: v.reduce_max(GT2[:, g:g + 1], TMP[:, 0:8], axis=AX.X))
            P.q("vector", lambda v: v.tensor_tensor(GSUM, GS, GT2, ALU.add))
            argmax_first(P, GSUM, iota32[:, 0:4], idxm32[:, 0:4], GEQ, TMP[:, 0:4], TB)
            P.q("vector", lambda v: v.tensor_copy(GMASK, GEQ))
            P.q("vector", lambda v: v.scalar_tensor_tensor(
                TMP[:, 0:4], GEQ, -1e30, GSUM, ALU.mult, ALU.add))
            argmax_first(P, TMP[:, 0:4], iota32[:, 0:4], idxm32[:, 0:4], GEQ, TMP[:, 4:8], TB)
            P.q("vector", lambda v: v.tensor_tensor(GMASK, GMASK, GEQ, ALU.add))
            gmask_b = bass.AP(RT, 216, [[256, 128], [1, 4], [0, 8]])
            P.q("vector", lambda v: v.tensor_tensor(
                CUR.rearrange("p (a b) -> p a b", a=4), CH.rearrange("p (a b) -> p a b", a=4),
                gmask_b, ALU.mult))
            for it in range(8):
                argmax_first(P, CUR, iota32[:], idxm32[:], EQ, TMP, TB)
                if it == 0:
                    P.q("vector", lambda v: v.tensor_copy(SELM, EQ))
                else:
                    P.q("vector", lambda v: v.tensor_tensor(SELM, SELM, EQ, ALU.add))
                if it < 7:
                    P.q("vector", lambda v: v.scalar_tensor_tensor(
                        CUR, EQ, -1e30, CUR, ALU.mult, ALU.add))
            P.q("vector", lambda v: v.tensor_tensor(TMP, SELM, SC, ALU.mult))
            P.q("vector", lambda v: v.reduce_sum(TB[:, 2:3], TMP, axis=AX.X))
            P.q("vector", lambda v: v.tensor_scalar(TB[:, 2:3], TB[:, 2:3], 1e-20, None, ALU.add))
            P.q("vector", lambda v: v.reciprocal(TB[:, 3:4], TB[:, 2:3]))
            P.q("vector", lambda v: v.tensor_scalar(TB[:, 3:4], TB[:, 3:4], 2.5, None, ALU.mult))
            P.q("vector", lambda v, i=i: v.tensor_scalar(CW[:, i, :], TMP, TB[:, 3:4], None, ALU.mult))

        for e in range(32):
            P.qdma("gpsimd", WG[:, :, :], w["L1f_eg_t"][e].rearrange("(b p) n -> p b n", p=128))
            P.qdma("gpsimd", WU[:, :, :], w["L1f_eu_t"][e].rearrange("(b p) n -> p b n", p=128))
            P.qdma("gpsimd", WD[:, :, :], w["L1f_ed_t"][e].rearrange("(b p) n -> p b n", p=128))
            for i in range(NT):
                P.qmm(PA[:, 0:512], [(XC[:, k, i * 128:(i + 1) * 128], WG[:, k, :])
                                     for k in range(ND)])
                P.q("scalar", lambda sc: sc.activation(HG[:], PA[:, 0:512], AF.Silu))
                P.qmm(PB2[:, 0:512], [(XC[:, k, i * 128:(i + 1) * 128], WU[:, k, :])
                                      for k in range(ND)])
                P.q("vector", lambda v: v.tensor_tensor(HID[:], PB2[:, 0:512], HG[:], ALU.mult))
                pbt = C.psum("PBtr", [128, 1024], BF16)

                def htr(t, pbt=pbt):
                    inst = None
                    for j in range(4):
                        inst = t.transpose(pbt[:, j * 128:(j + 1) * 128],
                                           HID[:, j * 128:(j + 1) * 128], ident[:])
                    return inst
                P.q("tensor", htr)
                P.q("vector", lambda v, pbt=pbt: v.tensor_copy(HIDC[:, :, :], pbt[:, 0:512]))
                for n in range(2):
                    P.qmm(PA[:, 0:512], [(HIDC[:, kb, :], WD[:, kb, n * 512:(n + 1) * 512])
                                         for kb in range(4)])
                    sl = slice(n * 512, (n + 1) * 512)
                    if e == 0:
                        P.q("vector", lambda v, i=i, sl=sl, e=e: v.tensor_scalar(
                            ACC[:, i, sl], PA[:, 0:512], CW[:, i, e:e + 1], None, ALU.mult))
                    else:
                        P.q("vector", lambda v, i=i, sl=sl, e=e: v.scalar_tensor_tensor(
                            ACC[:, i, sl], PA[:, 0:512], CW[:, i, e:e + 1],
                            ACC[:, i, sl], ALU.mult, ALU.add))

        for half in range(2):
            c0 = half * 512
            P.qdma("gpsimd", WG[:, :, :],
                   w["L1f_sg_t"][:, c0:c0 + 512].rearrange("(b p) n -> p b n", p=128))
            P.qdma("gpsimd", WU[:, :, :],
                   w["L1f_su_t"][:, c0:c0 + 512].rearrange("(b p) n -> p b n", p=128))
            P.qdma("gpsimd", WD[:, :, :],
                   w["L1f_sd_t"][c0:c0 + 512, :].rearrange("(b p) n -> p b n", p=128))
            for i in range(NT):
                P.qmm(PA[:, 0:512], [(XC[:, k, i * 128:(i + 1) * 128], WG[:, k, :])
                                     for k in range(ND)])
                P.q("scalar", lambda sc: sc.activation(HG[:], PA[:, 0:512], AF.Silu))
                P.qmm(PB2[:, 0:512], [(XC[:, k, i * 128:(i + 1) * 128], WU[:, k, :])
                                      for k in range(ND)])
                P.q("vector", lambda v: v.tensor_tensor(HID[:], PB2[:, 0:512], HG[:], ALU.mult))
                pbt = C.psum("PBtr", [128, 1024], BF16)

                def htr(t, pbt=pbt):
                    inst = None
                    for j in range(4):
                        inst = t.transpose(pbt[:, j * 128:(j + 1) * 128],
                                           HID[:, j * 128:(j + 1) * 128], ident[:])
                    return inst
                P.q("tensor", htr)
                P.q("vector", lambda v, pbt=pbt: v.tensor_copy(HIDC[:, :, :], pbt[:, 0:512]))
                for n in range(2):
                    P.qmm(PA[:, 0:512], [(HIDC[:, kb, :], WD[:, kb, n * 512:(n + 1) * 512])
                                         for kb in range(4)])
                    sl = slice(n * 512, (n + 1) * 512)
                    P.q("vector", lambda v, i=i, sl=sl: v.tensor_tensor(
                        XT[:, i, sl], XT[:, i, sl], PA[:, 0:512], ALU.add))
        for i in range(NT):
            P.q("vector", lambda v, i=i: v.tensor_tensor(XT[:, i, :], XT[:, i, :],
                                                         ACC[:, i, :], ALU.add))
            postln_tile(P, C, XT[:, i, :], STAT, SCR[:, :], eps=1e-5)
        P.qdma("sync", x_out[s].rearrange("(i p) d -> p i d", p=128), XT[:, :, :])


def build_L2a(nc, es, P, x_in, x_out, w, shared):
    C = Ctx(P, es, "B4_", shared)
    ident = C.t["ident"]
    XT = C.sbuf("XT", [128, NT, D], F32)
    XB = C.sbuf("XB", [128, D], BF16)
    XC = C.sbuf("XC", [128, ND, T], BF16)
    WP = C.sbuf("WP", [128, ND, 384], BF16)
    WO = C.sbuf("WO", [128, ND, D], BF16)
    QPC = C.sbuf("QPC", [128, T], BF16)
    KPC = C.sbuf("KPC", [128, T], BF16)
    VPT = C.sbuf("VPT", [128, NT, 128], BF16)
    SS = C.sbuf("SS", [128, T], F32)
    PP = C.sbuf("PP", [128, T], BF16)
    PT = C.sbuf("PT", [128, T], BF16)
    CTX = C.sbuf("CTX", [128, ND, T], BF16)
    STAT = C.sbuf("STAT", [128, 64], F32)
    SCR = C.sbuf("SCR", [128, D], F32)
    PA = C.psum("PA", [128, 512], F32)
    PB2 = C.psum("PB2", [128, 512], F32)
    PC = C.psum("PC", [128, 128], F32)

    P.qdma("gpsimd", WO[:, :, :], w["L2_wout"].rearrange("(b p) n -> p b n", p=128))

    for s in range(S_PER_CORE):
        P.qdma("sync", XT[:, :, :], x_in[s].rearrange("(i p) d -> p i d", p=128))
        for i in range(NT):
            P.q("vector", lambda v, i=i: v.tensor_copy(XB[:], XT[:, i, :]))
            tok_to_ch(P, C, XB[:], XC[:, :, i * 128:(i + 1) * 128])

        for p in range(8):
            P.qdma("gpsimd", WP[:, :, :], w["L2_wqkv_pg"][p].rearrange("(b p) n -> p b n", p=128))
            for dst, col0 in ((QPC, 0), (KPC, 128)):
                for c in range(4):
                    P.qmm(PA[:, 0:512],
                          [(WP[:, k, col0:col0 + 128], XC[:, k, c * 512:(c + 1) * 512])
                           for k in range(ND)])
                    P.q("vector", lambda v, dst=dst, c=c: v.tensor_copy(
                        dst[:, c * 512:(c + 1) * 512], PA[:, 0:512]))
            for i in range(NT):
                P.qmm(PC[:, 0:128],
                      [(XC[:, k, i * 128:(i + 1) * 128], WP[:, k, 256:384])
                       for k in range(ND)])
                P.q("scalar", lambda sc, i=i: sc.copy(VPT[:, i, :], PC[:, 0:128]))

            for hh in range(2):
                qsl = slice(64 * hh, 64 * hh + 64)
                for i in range(NT):
                    for c in range(4):
                        k0, k1 = c * 512, (c + 1) * 512
                        pa = PA if c % 2 == 0 else PB2
                        P.qmm(pa[:, 0:512],
                              [(QPC[qsl, i * 128:(i + 1) * 128], KPC[qsl, k0:k1])])
                        P.q("vector", lambda v, k0=k0, k1=k1, pa=pa: v.tensor_copy(
                            SS[:, k0:k1], pa[:, 0:512]))
                    P.q("vector", lambda v: v.reduce_max(
                        STAT[:, 8:9], SS[:, 0:T], axis=AX.X, negate=True))
                    P.q("vector", lambda v: v.tensor_scalar(
                        STAT[:, 9:10], STAT[:, 8:9], 1.0 / 32.0, None, ALU.mult))
                    P.q("scalar", lambda sc: sc.activation(
                        PP[:, 0:T], SS[:, 0:T], AF.Exp, bias=STAT[:, 9:10], scale=1.0 / 32.0,
                        accum_out=STAT[:, 10:11]))
                    P.q("vector", lambda v: v.reciprocal(STAT[:, 11:12], STAT[:, 10:11]))
                    P.q("vector", lambda v: v.tensor_scalar(
                        PP[:, 0:T], PP[:, 0:T], STAT[:, 11:12], None, ALU.mult))
                    for r0 in (0, 8):
                        pbt = C.psum("PBtr", [128, 1024], BF16)

                        def ptr(t, r0=r0, pbt=pbt):
                            inst = None
                            for j in range(r0, r0 + 8):
                                inst = t.transpose(pbt[:, (j - r0) * 128:(j - r0 + 1) * 128],
                                                   PP[:, j * 128:(j + 1) * 128], ident[:])
                            return inst
                        P.q("tensor", ptr)
                        P.q("vector", lambda v, r0=r0, pbt=pbt: v.tensor_copy(
                            PT[:, r0 * 128:(r0 + 8) * 128], pbt[:, 0:1024]))
                    P.qmm(PC[64 * hh:64 * hh + 64, 0:128],
                          [(VPT[:, j, qsl], PT[:, j * 128:(j + 1) * 128]) for j in range(NT)],
                          tile_position=(0, 64 * hh))
                    P.q("scalar", lambda sc, p=p, i=i, hh=hh: sc.copy(
                        CTX[64 * hh:64 * hh + 64, p, i * 128:(i + 1) * 128],
                        PC[64 * hh:64 * hh + 64, 0:128]))

        for i in range(NT):
            for n in range(2):
                P.qmm(PA[:, 0:512],
                      [(CTX[:, k, i * 128:(i + 1) * 128], WO[:, k, n * 512:(n + 1) * 512])
                       for k in range(ND)])
                sl = slice(n * 512, (n + 1) * 512)
                P.q("vector", lambda v, i=i, sl=sl: v.tensor_tensor(
                    XT[:, i, sl], XT[:, i, sl], PA[:, 0:512], ALU.add))
            postln_tile(P, C, XT[:, i, :], STAT, SCR[:, :], eps=1e-5)
        P.qdma("sync", x_out[s].rearrange("(i p) d -> p i d", p=128), XT[:, :, :])


def build_L2f(nc, es, P, x_in, x_out, w, shared):
    C = Ctx(P, es, "B5_", shared)
    ident = C.t["ident"]
    PD = 16
    W2C_W = PD + T + PD
    X2T = C.sbuf("X2T", [128, NT, D], F32)
    XB = C.sbuf("XB", [128, D], BF16)
    STAT = C.sbuf("STAT", [128, 64], F32)
    SCR = C.sbuf("SCR", [128, D], F32)
    ARE = C.sbuf("ARE", [128, 99 * 512], BF16)

    def V(off_kb, shape, dtype=BF16):
        esz = 2 if dtype == BF16 else 4
        n = int(np.prod(shape[1:]))
        ap = ARE.ap()[:, off_kb * 512: off_kb * 512 + (n * esz) // 2]
        if dtype == F32:
            ap = ap.bitcast(F32)
        if len(shape) == 3:
            ap = ap.rearrange("p (a b) -> p a b", a=shape[1])
        return ap

    X2C = V(0, [128, ND, W2C_W])
    WQKVP = V(33, [128, ND, 384])
    QTP = V(39, [128, NT, 128])
    KTP = V(43, [128, NT, 128])
    VTP = V(47, [128, NT, 128])
    QPC = V(51, [128, T])
    KPC = V(55, [128, T])
    RCQ = V(59, [128, 128], F32)
    RSQ = V(60, [128, 128], F32)
    RCK = V(61, [128, 128], F32)
    RSK = V(62, [128, 128], F32)
    QST = V(63, [128, 128], F32)
    KST = V(64, [128, 128], F32)
    MASKB = V(65, [128, 1024], F32)
    SS = V(69, [128, 1024], F32)
    PP = V(73, [128, 1024])
    PT = V(75, [128, 1024])
    CTX2 = V(77, [128, 4, T])
    RA = V(93, [128, 128], F32)
    RB = V(94, [128, 128], F32)
    WO2 = V(33, [128, 4, D])
    HT = V(0, [128, NT, D])
    NC2 = V(33, [128, ND, T])
    W1A = V(66, [128, ND, 512])
    W1G = V(74, [128, ND, 512])
    W2CH = V(82, [128, 4, D])
    AB = V(90, [128, 512], F32)
    GB = V(92, [128, 512])
    TB2 = V(93, [128, 512])
    TC = V(94, [128, 4, 128])
    B1A = V(95, [128, 512], F32)
    B1G = V(97, [128, 512], F32)
    B2T = V(69, [128, D], F32)
    LG = V(73, [128, D], F32)
    LB = V(77, [128, D], F32)

    PA = C.psum("PA", [128, 512], F32)
    PB2 = C.psum("PB2", [128, 512], F32)
    PC = C.psum("PC", [128, 128], F32)

    HALF = np.float32(0.5)

    def rothalf(src_ap, dst_ap):
        sh = bass.AP(src_ap.tensor, src_ap.offset, src_ap.ap)
        d0 = dst_ap.rearrange("p (h c) -> p h c", h=2)
        s0 = src_ap.rearrange("p (h c) -> p h c", h=2)
        P.q("vector", lambda v: v.tensor_scalar(d0[:, :, 0:32], s0[:, :, 32:64],
                                                -1.0, None, ALU.mult))
        P.q("vector", lambda v: v.tensor_copy(d0[:, :, 32:64], s0[:, :, 0:32]))

    for s in range(S_PER_CORE):
        P.qdma("sync", X2T[:, :, :], x_in[s].rearrange("(i p) d -> p i d", p=128))
        P.q("gpsimd", lambda g: g.memset(X2C[:, :, 0:PD], 0.0))
        P.q("gpsimd", lambda g: g.memset(X2C[:, :, PD + T:], 0.0))

        for l in range(2):
            for i in range(NT):
                P.q("vector", lambda v, i=i: v.tensor_copy(XB[:], X2T[:, i, :]))
                tok_to_ch(P, C, XB[:], X2C[:, :, PD + i * 128:PD + (i + 1) * 128])
            P.qdma("gpsimd", QST, w["L2f_qs"][l])
            P.qdma("gpsimd", KST, w["L2f_ks"][l])

            def shlhs(k, i):
                off = PD + i * 128 - (1 if k >= 4 else 0)
                return X2C[:, k, off:off + 128]

            for hp in range(4):
                P.qdma("gpsimd", WQKVP, w["L2f_wqkv_pg"][l, hp].rearrange("(b p) n -> p b n", p=128))
                for i in range(NT):
                    P.qdma("gpsimd", RCQ, w["L2f_cq"][i])
                    P.qdma("gpsimd", RSQ, w["L2f_sq"][i])
                    P.qdma("gpsimd", RCK, w["L2f_ck"][i])
                    P.qdma("gpsimd", RSK, w["L2f_sk"][i])
                    for which, dstT, col0 in (("q", QTP, 0), ("k", KTP, 128), ("v", VTP, 256)):
                        P.qmm(PC[:, 0:128], [(shlhs(k, i), WQKVP[:, k, col0:col0 + 128])
                                             for k in range(ND)])
                        if which == "v":
                            P.q("scalar", lambda sc, i=i: sc.copy(VTP[:, i, :], PC[:, 0:128]))
                            continue
                        for hh2 in range(2):
                            hs = slice(64 * hh2, 64 * hh2 + 64)
                            P.q("scalar", lambda sc, hs=hs: sc.activation(
                                SCR[:, 0:64], PC[:, hs], AF.Square, accum_out=STAT[:, 4:5]))
                            P.q("scalar", lambda sc: sc.activation(
                                STAT[:, 5:6], STAT[:, 4:5], AF.Sqrt, bias=C.cs(0.0), scale=1.0))
                            P.q("vector", lambda v: v.tensor_scalar(
                                STAT[:, 5:6], STAT[:, 5:6], 1e-12, None, ALU.max))
                            P.q("vector", lambda v: v.reciprocal(STAT[:, 6:7], STAT[:, 5:6]))
                            P.q("vector", lambda v, hs=hs: v.tensor_scalar(
                                RA[:, hs], PC[:, hs], STAT[:, 6:7], None, ALU.mult))
                        sct = QST if which == "q" else KST
                        P.q("vector", lambda v, sct=sct: v.tensor_tensor(RA, RA, sct, ALU.mult))
                        rothalf(RA, RB)
                        ctab = RCQ if which == "q" else RCK
                        stab = RSQ if which == "q" else RSK
                        P.q("vector", lambda v, ctab=ctab: v.tensor_tensor(RA, RA, ctab, ALU.mult))
                        P.q("vector", lambda v, stab=stab: v.tensor_tensor(RB, RB, stab, ALU.mult))
                        P.q("vector", lambda v, i=i, dstT=dstT: v.tensor_tensor(
                            dstT[:, i, :], RA, RB, ALU.add))
                for i in range(NT):
                    pbt = C.psum("PBtr", [128, 1024], BF16)
                    P.q("tensor", lambda t, i=i, pbt=pbt: t.transpose(pbt[:, 0:128], QTP[:, i, :], ident[:]))
                    P.q("vector", lambda v, i=i, pbt=pbt: v.tensor_copy(
                        QPC[:, i * 128:(i + 1) * 128], pbt[:, 0:128]))
                    P.q("tensor", lambda t, i=i, pbt=pbt: t.transpose(pbt[:, 128:256], KTP[:, i, :], ident[:]))
                    P.q("vector", lambda v, i=i, pbt=pbt: v.tensor_copy(
                        KPC[:, i * 128:(i + 1) * 128], pbt[:, 128:256]))

                for hh in range(2):
                    qsl = slice(64 * hh, 64 * hh + 64)
                    for win in range(4):
                        for sdx in range(4):
                            i = 4 * win + sdx
                            P.qdma("gpsimd", MASKB, w["c_l2f_mask"][sdx])
                            if win == 0:
                                nk, ktile0 = 512, 0
                                P.qmm(PA[:, 0:512],
                                      [(QPC[qsl, i * 128:(i + 1) * 128], KPC[qsl, 0:512])])
                                P.q("vector", lambda v: v.tensor_tensor(
                                    SS[:, 0:512], PA[:, 0:512], MASKB[:, 512:1024], ALU.add))
                            else:
                                nk, ktile0 = 1024, 4 * (win - 1)
                                k0t = 512 * (win - 1)
                                for c in range(2):
                                    pa = PA if c == 0 else PB2
                                    P.qmm(pa[:, 0:512],
                                          [(QPC[qsl, i * 128:(i + 1) * 128],
                                            KPC[qsl, k0t + c * 512:k0t + (c + 1) * 512])])
                                    P.q("vector", lambda v, c=c, pa=pa: v.tensor_tensor(
                                        SS[:, c * 512:(c + 1) * 512], pa[:, 0:512],
                                        MASKB[:, c * 512:(c + 1) * 512], ALU.add))
                            P.q("vector", lambda v, nk=nk: v.reduce_max(
                                STAT[:, 8:9], SS[:, 0:nk], axis=AX.X, negate=True))
                            P.q("scalar", lambda sc, nk=nk: sc.activation(
                                PP[:, 0:nk], SS[:, 0:nk], AF.Exp, bias=STAT[:, 8:9],
                                scale=1.0, accum_out=STAT[:, 10:11]))
                            P.q("vector", lambda v: v.reciprocal(STAT[:, 11:12], STAT[:, 10:11]))
                            P.q("vector", lambda v, nk=nk: v.tensor_scalar(
                                PP[:, 0:nk], PP[:, 0:nk], STAT[:, 11:12], None, ALU.mult))
                            nkt = nk // 128
                            pbt = C.psum("PBtr", [128, 1024], BF16)

                            def ptr(t, nkt=nkt, pbt=pbt):
                                inst = None
                                for j in range(nkt):
                                    inst = t.transpose(pbt[:, j * 128:(j + 1) * 128],
                                                       PP[:, j * 128:(j + 1) * 128], ident[:])
                                return inst
                            P.q("tensor", ptr)
                            P.q("vector", lambda v, nkt=nkt, pbt=pbt: v.tensor_copy(
                                PT[:, 0:nkt * 128], pbt[:, 0:nkt * 128]))
                            P.qmm(PC[64 * hh:64 * hh + 64, 0:128],
                                  [(VTP[:, ktile0 + j, qsl], PT[:, j * 128:(j + 1) * 128])
                                   for j in range(nkt)],
                                  tile_position=(0, 64 * hh))
                            P.q("scalar", lambda sc, hp=hp, i=i, hh=hh: sc.copy(
                                CTX2[64 * hh:64 * hh + 64, hp, i * 128:(i + 1) * 128],
                                PC[64 * hh:64 * hh + 64, 0:128]))

            P.qdma("gpsimd", WO2, w["L2f_wo"][l].rearrange("(b p) n -> p b n", p=128))
            for i in range(NT):
                for n in range(2):
                    P.qmm(PA[:, 0:512],
                          [(CTX2[:, kb, i * 128:(i + 1) * 128], WO2[:, kb, n * 512:(n + 1) * 512])
                           for kb in range(4)])
                    sl = slice(n * 512, (n + 1) * 512)
                    P.q("vector", lambda v, i=i, sl=sl: v.tensor_tensor(
                        X2T[:, i, sl], X2T[:, i, sl], PA[:, 0:512], ALU.add))

            P.qdma("gpsimd", LG, w["L2f_lng"][l])
            P.qdma("gpsimd", LB, w["L2f_lnb"][l])
            for i in range(NT):
                P.q("vector", lambda v, i=i: v.tensor_copy(HT[:, i, 0:512], X2T[:, i, 0:512]))
                P.qdma("gpsimd", HT[1:128, i, 512:1024], X2T[0:127, i, 512:1024])
                if i == 0:
                    P.q("gpsimd", lambda g: g.memset(HT[0:1, 0, 512:1024], 0.0))
                else:
                    P.qdma("gpsimd", HT[0:1, i, 512:1024], X2T[127:128, i - 1, 512:1024])
            for i in range(NT):
                postln_tile(P, C, HT[:, i, :], STAT, SCR[:, :], eps=1e-5,
                            gamma=LG, beta=LB, out_ap=XB[:])
                tok_to_ch(P, C, XB[:], NC2[:, :, i * 128:(i + 1) * 128])

            P.qdma("gpsimd", B2T, w["L2f_b2"][l])
            chunks = [(c, 512) for c in range(0, 2560, 512)] + [(2560, 170)]
            for (c0, cs) in chunks:
                P.qdma("gpsimd", W1A[:, :, 0:cs].opt() if cs != 512 else W1A,
                       w["L2f_w1"][l][:, c0:c0 + cs].rearrange("(b p) n -> p b n", p=128))
                P.qdma("gpsimd", W1G[:, :, 0:cs].opt() if cs != 512 else W1G,
                       w["L2f_w1"][l][:, 2730 + c0:2730 + c0 + cs].rearrange("(b p) n -> p b n", p=128))
                subs = [(r, min(128, cs - r)) for r in range(0, cs, 128)]
                for (r0, rs) in subs:
                    P.qdma("gpsimd", W2CH[0:rs, r0 // 128, :],
                           w["L2f_w2"][l][c0 + r0:c0 + r0 + rs, :])
                P.qdma("gpsimd", B1A[:, 0:cs], w["L2f_b1"][l][:, c0:c0 + cs])
                P.qdma("gpsimd", B1G[:, 0:cs], w["L2f_b1"][l][:, 2730 + c0:2730 + c0 + cs])
                for i in range(NT):
                    P.qmm(PA[:, 0:cs], [(NC2[:, k, i * 128:(i + 1) * 128], W1A[:, k, 0:cs])
                                        for k in range(ND)])
                    P.q("vector", lambda v, cs=cs: v.tensor_tensor(
                        AB[:, 0:cs], PA[:, 0:cs], B1A[:, 0:cs], ALU.add))
                    P.qmm(PB2[:, 0:cs], [(NC2[:, k, i * 128:(i + 1) * 128], W1G[:, k, 0:cs])
                                         for k in range(ND)])
                    P.q("vector", lambda v, cs=cs: v.tensor_tensor(
                        SCR[:, 0:cs], PB2[:, 0:cs], B1G[:, 0:cs], ALU.add))
                    P.q("scalar", lambda sc, cs=cs: sc.activation(GB[:, 0:cs], SCR[:, 0:cs], AF.Gelu))
                    P.q("vector", lambda v, cs=cs: v.tensor_tensor(
                        TB2[:, 0:cs], AB[:, 0:cs], GB[:, 0:cs], ALU.mult))
                    pbt = C.psum("PBtr", [128, 1024], BF16)

                    def htr(t, subs=subs, pbt=pbt):
                        inst = None
                        for si, (r0, rs) in enumerate(subs):
                            inst = t.transpose(pbt[0:rs, si * 128:(si + 1) * 128],
                                               TB2[:, r0:r0 + rs], ident[:])
                        return inst
                    P.q("tensor", htr)
                    for si, (r0, rs) in enumerate(subs):
                        P.q("vector", lambda v, si=si, rs=rs, pbt=pbt: v.tensor_copy(
                            TC[0:rs, si, :], pbt[0:rs, si * 128:(si + 1) * 128]))
                    for n in range(2):
                        P.qmm(PA[:, 0:512],
                              [(TC[0:rs, si, :], W2CH[0:rs, si, n * 512:(n + 1) * 512])
                               for si, (r0, rs) in enumerate(subs)])
                        sl = slice(n * 512, (n + 1) * 512)
                        P.q("vector", lambda v, i=i, sl=sl: v.tensor_tensor(
                            X2T[:, i, sl], X2T[:, i, sl], PA[:, 0:512], ALU.add))
            for i in range(NT):
                P.q("vector", lambda v, i=i: v.tensor_tensor(
                    X2T[:, i, :], X2T[:, i, :], B2T, ALU.add))

        for i in range(NT):
            P.qdma("sync", SCR[:, :], x_in[s][i * 128:(i + 1) * 128, :])
            P.q("vector", lambda v, i=i: v.tensor_tensor(
                X2T[:, i, :], X2T[:, i, :], SCR[:, :], ALU.add))
            postln_tile(P, C, X2T[:, i, :], STAT, SCR[:, :], eps=1e-5)
        P.qdma("sync", x_out[s].rearrange("(i p) d -> p i d", p=128), X2T[:, :, :])


def build_L3a(nc, es, P, x_in, x_out, w, shared):
    C = Ctx(P, es, "B6_", shared)
    ident = C.t["ident"]
    ones = C.t["ones"]
    PD = 16
    wdwc = C.sbuf("wdwc", [128, 8, 3], F32)
    bdwc = C.sbuf("bdwc", [128, 8], F32)
    bproj = C.sbuf("bproj", [128, D], F32)
    XT = C.sbuf("XT", [128, NT, D], F32)
    XB = C.sbuf("XB", [128, D], BF16)
    STAT = C.sbuf("STAT", [128, 64], F32)
    SCR = C.sbuf("SCR", [128, D], F32)
    ARE = C.sbuf("ARE", [128, 100 * 512], BF16)

    def V(off_kb, shape, dtype=BF16):
        esz = 2 if dtype == BF16 else 4
        n = int(np.prod(shape[1:]))
        ap = ARE.ap()[:, off_kb * 512: off_kb * 512 + (n * esz) // 2]
        if dtype == F32:
            ap = ap.bitcast(F32)
        if len(shape) == 3:
            ap = ap.rearrange("p (a b) -> p a b", a=shape[1])
        return ap

    XC = V(0, [128, ND, T])
    OUTC = V(33, [128, ND, PD + T + PD])
    WP = V(67, [128, ND, 384])
    QF = V(73, [128, NT, 128], F32)
    KN = V(81, [128, NT, 128])
    VTP = V(85, [128, NT, 128])
    QZT = V(89, [128, NT, 128])
    QZC = V(93, [128, T])
    KV = V(97, [128, 64])
    KSB = V(98, [128, 128], F32)
    OUTSUM = V(0, [128, ND, T])
    WPJ = V(67, [128, ND, D])
    CS = V(83, [128, T], F32)
    PA = C.psum("PA", [128, 512], F32)
    PC = C.psum("PC", [128, 128], F32)
    PD2 = C.psum("PD2", [128, 128], F32)

    P.qdma("gpsimd", wdwc[:, :, :], w["L3a_wdwc"])
    P.qdma("gpsimd", bdwc[:, :], w["L3a_bdwc"])
    P.qdma("gpsimd", bproj[:], w["L3a_bproj"])

    for s in range(S_PER_CORE):
        P.qdma("sync", XT[:, :, :], x_in[s].rearrange("(i p) d -> p i d", p=128))
        P.q("gpsimd", lambda g: g.memset(OUTC[:, :, 0:PD], 0.0))
        P.q("gpsimd", lambda g: g.memset(OUTC[:, :, PD + T:], 0.0))
        for i in range(NT):
            P.q("vector", lambda v, i=i: v.tensor_copy(XB[:], XT[:, i, :]))
            tok_to_ch(P, C, XB[:], XC[:, :, i * 128:(i + 1) * 128])

        for p in range(8):
            P.qdma("gpsimd", WP, w["L3a_wqkv_pg"][p].rearrange("(b p) n -> p b n", p=128))
            for which, col0 in (("q", 0), ("k", 128), ("v", 256)):
                for i in range(NT):
                    P.qmm(PC[:, 0:128],
                          [(XC[:, k, i * 128:(i + 1) * 128], WP[:, k, col0:col0 + 128])
                           for k in range(ND)])
                    if which == "v":
                        P.q("scalar", lambda sc, i=i: sc.copy(VTP[:, i, :], PC[:, 0:128]))
                        continue
                    for hh in range(2):
                        hs = slice(64 * hh, 64 * hh + 64)
                        P.q("scalar", lambda sc, hs=hs: sc.activation(
                            SCR[:, 0:64], PC[:, hs], AF.Relu))
                        P.q("vector", lambda v: v.tensor_scalar(
                            SCR[:, 0:64], SCR[:, 0:64], C.cs(1e-6), None, ALU.add))
                        P.q("vector", lambda v: v.reduce_sum(STAT[:, 0:1], SCR[:, 0:64], axis=AX.X))
                        P.q("vector", lambda v: v.reciprocal(STAT[:, 1:2], STAT[:, 0:1]))
                        P.q("vector", lambda v: v.tensor_scalar(
                            SCR[:, 0:64], SCR[:, 0:64], STAT[:, 1:2], None, ALU.mult))
                        P.q("scalar", lambda sc: sc.activation(
                            SCR[:, 64:128], SCR[:, 0:64], AF.Square))
                        P.q("vector", lambda v: v.tensor_tensor(
                            SCR[:, 64:128], SCR[:, 64:128], SCR[:, 0:64], ALU.mult))
                        P.q("vector", lambda v: v.reduce_sum(STAT[:, 2:3], SCR[:, 64:128], axis=AX.X))
                        P.q("vector", lambda v: v.reciprocal(STAT[:, 3:4], STAT[:, 2:3]))
                        if which == "q":
                            P.q("vector", lambda v, i=i, hs=hs: v.tensor_scalar(
                                QF[:, i, hs], SCR[:, 64:128], STAT[:, 3:4], None, ALU.mult))
                        else:
                            P.q("vector", lambda v, i=i, hs=hs: v.tensor_scalar(
                                KN[:, i, hs], SCR[:, 64:128], STAT[:, 3:4], None, ALU.mult))
            for hh in range(2):
                hs = slice(64 * hh, 64 * hh + 64)
                P.qmm(PD2[64 * hh:64 * hh + 64, 0:64],
                      [(KN[:, j, hs], VTP[:, j, hs]) for j in range(NT)],
                      tile_position=(0, 64 * hh))
            P.q("vector", lambda v: v.tensor_copy(KV[:, 0:64], PD2[:, 0:64]))
            P.qmm(PC[0:1, 0:128], [(ones[:, 0:1], KN[:, j, :]) for j in range(NT)])
            P.q("vector", lambda v: v.tensor_copy(KSB[0:1, :], PC[0:1, 0:128]))
            P.q("vector", lambda v: v.tensor_copy(XB[0:1, 0:128], KSB[0:1, :]))
            P.qmm(PC[:, 0:128], [(ones[0:1, :], XB[0:1, 0:128])])
            P.q("vector", lambda v: v.tensor_copy(KSB[:, :], PC[:, 0:128]))
            for i in range(NT):
                P.q("vector", lambda v, i=i: v.tensor_tensor(
                    SCR[:, 0:128], QF[:, i, :], KSB[:, :], ALU.mult))
                for hh in range(2):
                    hs = slice(64 * hh, 64 * hh + 64)
                    P.q("vector", lambda v, hs=hs: v.reduce_sum(
                        STAT[:, 4:5], SCR[:, hs], axis=AX.X))
                    P.q("vector", lambda v: v.tensor_scalar(
                        STAT[:, 4:5], STAT[:, 4:5], C.cs(1e-6), None, ALU.add))
                    P.q("vector", lambda v: v.reciprocal(STAT[:, 5:6], STAT[:, 4:5]))
                    P.q("vector", lambda v, i=i, hs=hs: v.tensor_scalar(
                        QZT[:, i, hs], QF[:, i, hs], STAT[:, 5:6], None, ALU.mult))
            for i in range(NT):
                pbt = C.psum("PBtr", [128, 1024], BF16)
                P.q("tensor", lambda t, i=i, pbt=pbt: t.transpose(pbt[:, 0:128], QZT[:, i, :], ident[:]))
                P.q("vector", lambda v, i=i, pbt=pbt: v.tensor_copy(
                    QZC[:, i * 128:(i + 1) * 128], pbt[:, 0:128]))
            for hh in range(2):
                for c in range(4):
                    P.qmm(PA[64 * hh:64 * hh + 64, 0:512],
                          [(KV[64 * hh:64 * hh + 64, 0:64],
                            QZC[64 * hh:64 * hh + 64, c * 512:(c + 1) * 512])],
                          tile_position=(64 * hh, 64 * hh))
                    P.q("scalar", lambda sc, p=p, hh=hh, c=c: sc.copy(
                        OUTC[64 * hh:64 * hh + 64, p, PD + c * 512:PD + (c + 1) * 512],
                        PA[64 * hh:64 * hh + 64, 0:512]))

        for b in range(ND):
            P.q("vector", lambda v, b=b: v.tensor_scalar(
                CS[:, :], OUTC[:, b, PD - 1:PD - 1 + T], wdwc[:, b, 0:1], None, ALU.mult))
            P.q("vector", lambda v, b=b: v.scalar_tensor_tensor(
                CS[:, :], OUTC[:, b, PD:PD + T], wdwc[:, b, 1:2], CS[:, :], ALU.mult, ALU.add))
            P.q("vector", lambda v, b=b: v.scalar_tensor_tensor(
                CS[:, :], OUTC[:, b, PD + 1:PD + 1 + T], wdwc[:, b, 2:3], CS[:, :], ALU.mult, ALU.add))
            P.q("vector", lambda v, b=b: v.tensor_scalar(
                CS[:, :], CS[:, :], bdwc[:, b:b + 1], None, ALU.add))
            P.q("vector", lambda v, b=b: v.tensor_tensor(
                OUTSUM[:, b, :], CS[:, :], OUTC[:, b, PD:PD + T], ALU.add))

        P.qdma("gpsimd", WPJ, w["L3a_wproj_t"].rearrange("(b p) n -> p b n", p=128))
        for i in range(NT):
            for n in range(2):
                P.qmm(PA[:, 0:512],
                      [(OUTSUM[:, k, i * 128:(i + 1) * 128], WPJ[:, k, n * 512:(n + 1) * 512])
                       for k in range(ND)])
                sl = slice(n * 512, (n + 1) * 512)
                P.q("vector", lambda v, sl=sl: v.tensor_tensor(
                    SCR[:, sl], PA[:, 0:512], bproj[:, sl], ALU.add))
                P.q("vector", lambda v, i=i, sl=sl: v.tensor_tensor(
                    XT[:, i, sl], XT[:, i, sl], SCR[:, sl], ALU.add))
            postln_tile(P, C, XT[:, i, :], STAT, SCR[:, :], eps=1e-5)
        P.qdma("sync", x_out[s].rearrange("(i p) d -> p i d", p=128), XT[:, :, :])


def build_L3f(nc, es, P, x_in, x_out, w, shared):
    C = Ctx(P, es, "B7_", shared)
    ident = C.t["ident"]
    iota32 = C.t["iota32"]
    idxm32 = C.t["idxm32"]
    PD = 16
    convw = C.sbuf("convw", [128, 8, 4], F32)
    convb = C.sbuf("convb", [128, 8], F32)
    rbias = C.sbuf("rbias", [128, 8], F32)
    wrt = C.sbuf("wrt", [128, ND, 8], BF16)
    XT = C.sbuf("XT", [128, NT, D], F32)
    XB = C.sbuf("XB", [128, D], BF16)
    XC = C.sbuf("XCp", [128, ND, PD + T + PD], BF16)
    CONVC = C.sbuf("CONVC", [128, ND, T], BF16)
    CW = C.sbuf("CW", [128, NT, 8], F32)
    RT = C.sbuf("RT", [128, 64], F32)
    WG = C.sbuf("WG", [128, ND, 512], BF16)
    WU = C.sbuf("WU", [128, ND, 512], BF16)
    WD = C.sbuf("WD", [128, 4, D], BF16)
    HG = C.sbuf("HG", [128, 512], BF16)
    HID = C.sbuf("HID", [128, 512], BF16)
    HIDC = C.sbuf("HIDC", [128, 4, 128], BF16)
    STAT = C.sbuf("STAT", [128, 64], F32)
    SCR = C.sbuf("SCR", [128, D], F32)
    CS = C.sbuf("CS", [128, T], F32)
    PA = C.psum("PA", [128, 512], F32)
    PB2 = C.psum("PB2", [128, 512], F32)

    P.qdma("gpsimd", convw[:, :, :], w["L3f_convw"])
    P.qdma("gpsimd", convb[:, :], w["L3f_convb"])
    P.qdma("gpsimd", rbias[:], w["L3f_rbias"])
    P.qdma("gpsimd", wrt[:, :, :], w["L3f_wr"].rearrange("(b p) n -> p b n", p=128))

    for s in range(S_PER_CORE):
        P.qdma("sync", XT[:, :, :], x_in[s].rearrange("(i p) d -> p i d", p=128))
        P.q("gpsimd", lambda g: g.memset(XC[:, :, 0:PD], 0.0))
        P.q("gpsimd", lambda g: g.memset(XC[:, :, PD + T:], 0.0))
        for i in range(NT):
            P.q("vector", lambda v, i=i: v.tensor_copy(XB[:], XT[:, i, :]))
            tok_to_ch(P, C, XB[:], XC[:, :, PD + i * 128:PD + (i + 1) * 128])

        SC8 = RT[:, 0:8]; CH8 = RT[:, 8:16]; EQ1 = RT[:, 16:24]; EQ2 = RT[:, 24:32]
        TMP8 = RT[:, 32:40]; TB = RT[:, 40:48]; SEL = RT[:, 48:56]
        for i in range(NT):
            P.qmm(PA[:, 0:8], [(XC[:, k, PD + i * 128:PD + (i + 1) * 128], wrt[:, k, :])
                               for k in range(ND)])
            P.q("scalar", lambda sc: sc.activation(SC8, PA[:, 0:8], AF.Sigmoid))
            P.q("vector", lambda v: v.tensor_tensor(CH8, SC8, rbias[:], ALU.add))
            argmax_first(P, CH8, iota32[:, 0:8], idxm32[:, 0:8], EQ1, TMP8, TB)
            P.q("vector", lambda v: v.scalar_tensor_tensor(
                TMP8, EQ1, -1e30, CH8, ALU.mult, ALU.add))
            argmax_first(P, TMP8, iota32[:, 0:8], idxm32[:, 0:8], EQ2, RT[:, 56:64], TB)
            P.q("vector", lambda v: v.tensor_tensor(SEL, EQ1, EQ2, ALU.add))
            P.q("vector", lambda v: v.tensor_tensor(TMP8, SEL, SC8, ALU.mult))
            P.q("vector", lambda v: v.reduce_sum(TB[:, 2:3], TMP8, axis=AX.X))
            P.q("vector", lambda v: v.tensor_scalar(TB[:, 2:3], TB[:, 2:3], C.cs(1e-9), None, ALU.add))
            P.q("vector", lambda v: v.reciprocal(TB[:, 3:4], TB[:, 2:3]))
            P.q("vector", lambda v, i=i: v.tensor_scalar(CW[:, i, :], TMP8, TB[:, 3:4], None, ALU.mult))

        for i in range(NT):
            P.q("vector", lambda v, i=i: v.tensor_tensor(
                STAT[:, 8:9], CW[:, i, 4:5], CW[:, i, 6:7], ALU.add))
            P.q("scalar", lambda sc, i=i: sc.activation(SCR[:], XT[:, i, :], AF.Relu))
            P.q("vector", lambda v, i=i: v.scalar_tensor_tensor(
                SCR[:], SCR[:], CW[:, i, 7:8], XT[:, i, :], ALU.mult, ALU.bypass)
                if False else v.tensor_scalar(SCR[:], SCR[:], CW[:, i, 7:8], None, ALU.mult))
            P.q("vector", lambda v, i=i: v.scalar_tensor_tensor(
                SCR[:], XT[:, i, :], STAT[:, 8:9], SCR[:], ALU.mult, ALU.add))
            P.q("vector", lambda v, i=i: v.tensor_tensor(
                XT[:, i, :], XT[:, i, :], SCR[:], ALU.add))

        for b in range(ND):
            P.q("vector", lambda v, b=b: v.tensor_scalar(
                CS[:, :], XC[:, b, PD - 3:PD - 3 + T], convw[:, b, 0:1], None, ALU.mult))
            for kk in range(1, 4):
                P.q("vector", lambda v, b=b, kk=kk: v.scalar_tensor_tensor(
                    CS[:, :], XC[:, b, PD - 3 + kk:PD - 3 + kk + T], convw[:, b, kk:kk + 1],
                    CS[:, :], ALU.mult, ALU.add))
            P.q("vector", lambda v, b=b: v.tensor_scalar(
                CONVC[:, b, :], CS[:, :], convb[:, b:b + 1], None, ALU.add))

        for e in range(4):
            P.qdma("gpsimd", WG[:, :, :], w["L3f_wg"][e].rearrange("(b p) n -> p b n", p=128))
            P.qdma("gpsimd", WU[:, :, :], w["L3f_wu"][e].rearrange("(b p) n -> p b n", p=128))
            P.qdma("gpsimd", WD[:, :, :], w["L3f_wd"][e].rearrange("(b p) n -> p b n", p=128))
            for i in range(NT):
                P.qmm(PA[:, 0:512], [(XC[:, k, PD + i * 128:PD + (i + 1) * 128], WG[:, k, :])
                                     for k in range(ND)])
                P.q("scalar", lambda sc: sc.activation(HG[:], PA[:, 0:512], AF.Silu))
                P.qmm(PB2[:, 0:512], [(XC[:, k, PD + i * 128:PD + (i + 1) * 128], WU[:, k, :])
                                      for k in range(ND)])
                P.q("vector", lambda v: v.tensor_tensor(HID[:], PB2[:, 0:512], HG[:], ALU.mult))
                pbt = C.psum("PBtr", [128, 1024], BF16)

                def htr(t, pbt=pbt):
                    inst = None
                    for j in range(4):
                        inst = t.transpose(pbt[:, j * 128:(j + 1) * 128],
                                           HID[:, j * 128:(j + 1) * 128], ident[:])
                    return inst
                P.q("tensor", htr)
                P.q("vector", lambda v, pbt=pbt: v.tensor_copy(HIDC[:, :, :], pbt[:, 0:512]))
                for n in range(2):
                    P.qmm(PA[:, 0:512], [(HIDC[:, kb, :], WD[:, kb, n * 512:(n + 1) * 512])
                                         for kb in range(4)])
                    sl = slice(n * 512, (n + 1) * 512)
                    P.q("vector", lambda v, i=i, sl=sl, e=e: v.scalar_tensor_tensor(
                        XT[:, i, sl], PA[:, 0:512], CW[:, i, e:e + 1],
                        XT[:, i, sl], ALU.mult, ALU.add))

        for i in range(NT):
            pbt = C.psum("PBtr", [128, 1024], BF16)

            def ctr(t, i=i, pbt=pbt):
                inst = None
                for b in range(ND):
                    inst = t.transpose(pbt[:, b * 128:(b + 1) * 128],
                                       CONVC[:, b, i * 128:(i + 1) * 128], ident[:])
                return inst
            P.q("tensor", ctr)
            P.q("vector", lambda v, i=i, pbt=pbt: v.tensor_tensor(
                XT[:, i, :], XT[:, i, :], pbt[:, 0:1024], ALU.add))
            postln_tile(P, C, XT[:, i, :], STAT, SCR[:, :], eps=1e-5)
        P.qdma("sync", x_out[s].rearrange("(i p) d -> p i d", p=128), XT[:, :, :])


BLOCKS = [build_L0a, build_L0f, build_L1a, build_L1f, build_L2a, build_L2f,
          build_L3a, build_L3f]

_cache = {}


def build_fused(nc, es):
    import contextlib
    P = Prog(nc, es)
    x_in = nc.dram_tensor("x_in", [S_PER_CORE, T, D], F32, kind="ExternalInput").ap()
    x_out = nc.dram_tensor("x_out", [S_PER_CORE, T, D], F32, kind="ExternalOutput").ap()
    xa = nc.dram_tensor("xa", [S_PER_CORE, T, D], F32, kind="Internal").ap()
    xb = nc.dram_tensor("xb", [S_PER_CORE, T, D], F32, kind="Internal").ap()
    w = {k: nc.dram_tensor(k, shp, F32, kind="ExternalInput").ap()
         for k, shp in WSHAPES.items()}
    shared = {
        "CONSTS": es.enter_context(nc.sbuf_tensor("CONSTS", [128, len(CONST_VALS)], F32)),
        "ident": es.enter_context(nc.sbuf_tensor("ident", [128, 128], BF16)),
        "causal": es.enter_context(nc.sbuf_tensor("causal", [128, 128], F32)),
        "mkeep": es.enter_context(nc.sbuf_tensor("mkeep", [128, 128], F32)),
        "mfill": es.enter_context(nc.sbuf_tensor("mfill", [128, 128], F32)),
        "iota32": es.enter_context(nc.sbuf_tensor("iota32", [128, 32], F32)),
        "idxm32": es.enter_context(nc.sbuf_tensor("idxm32", [128, 32], F32)),
        "ones": es.enter_context(nc.sbuf_tensor("ones", [128, 128], BF16)),
    }
    for name, key in (("CONSTS", "c_scalars"), ("ident", "c_ident"),
                      ("causal", "c_causal0"), ("mkeep", "c_strict_keep"),
                      ("mfill", "c_strict_fill"), ("iota32", "c_iota32"),
                      ("idxm32", "c_idxm32"), ("ones", "c_ones")):
        P.qdma("gpsimd", shared[name][:, :], w[key])
    hops = [x_in, xa, xb, xa, xb, xa, xb, xa, x_out]
    for k, bld in enumerate(BLOCKS):
        with contextlib.ExitStack() as bes:
            bld(nc, bes, P, hops[k], hops[k + 1], w, shared)
    P.finish()


def _get_program():
    if "fused" not in _cache:
        import contextlib
        es = contextlib.ExitStack()
        nc = bass.Bass("TRN2", target_bir_lowering=False)
        with es:
            build_fused(nc, es)
        _cache["fused"] = nc
    return _cache["fused"]


def kernel(**inputs):
    st = _stage(inputs)
    x = np.ascontiguousarray(np.asarray(inputs["L0_attn_x"], np.float32))
    shards = x.reshape(NCORES, S_PER_CORE, T, D)
    nc = _get_program()
    wmap = {n: np.ascontiguousarray(st[n].astype(np.float32)) for n in WSHAPES}
    in_maps = [{"x_in": np.ascontiguousarray(shards[c]), **wmap}
               for c in range(NCORES)]
    res = run_bass_kernel_spmd(nc, in_maps, core_ids=list(range(NCORES)))
    cur = [np.asarray(res.results[c]["x_out"]) for c in range(NCORES)]
    return np.concatenate(cur, axis=0).reshape(32, T, D)
